# version 15
# baseline (speedup 1.0000x reference)
import os
import numpy as np
from contextlib import ExitStack

import concourse.bass as bass
import concourse.mybir as mybir
from concourse.bass_utils import run_bass_kernel_spmd

F32 = mybir.dt.float32
BF16 = mybir.dt.bfloat16
AF = mybir.ActivationFunctionType
ALU = mybir.AluOpType
AX = mybir.AxisListType

D = 1024
L = 4096
HID = 4096
DC = 8
HC = 32
NP = 8
TT = 512
NT = L // TT
AB = 256
NAB = L // AB
LNEPS = 1e-5
QEPS = 1e-6



class Sem:
    def __init__(self, h, step):
        self.h = h
        self.step = step
        self.n = 0


class _NullInstr:
    def then_inc(self, *a, **k):
        return self


_NULL = _NullInstr()


class _EngProxy:
    def __init__(self, em, name):
        self._em = em
        self._name = name

    def __getattr__(self, meth):
        em = self._em
        name = self._name

        def call(*args, **kwargs):
            if em.mode == "real" and em.cur == name:
                return getattr(em.real_eng, meth)(*args, **kwargs)
            return _NULL

        return call


class Em:

    ENGS = ("pe", "ve", "sc", "sy", "gp")

    def __init__(self, sems):
        self.sems = sems
        self.mode = "dry"
        self.cur = None
        self.real_eng = None
        self.rec = {e: [] for e in self.ENGS}
        self.pos = {e: 0 for e in self.ENGS}
        self.proxies = {e: _EngProxy(self, e) for e in self.ENGS}
        self._snaps = []
        self._snap_pos = 0

    def e(self, name):
        return self.proxies[name]

    def inc(self, instr, sem):
        if self.mode == "dry":
            sem.n += sem.step
        elif instr is not _NULL:
            instr.then_inc(sem.h, sem.step)
        return instr

    def wait(self, eng_name, sem, val=None):
        if self.mode == "dry":
            v = sem.n if val is None else val
            self.rec[eng_name].append(v)
        else:
            v = self.rec[eng_name][self.pos[eng_name]]
            self.pos[eng_name] += 1
            if self.cur == eng_name and v > 0:
                self.real_eng.wait_ge(sem.h, v)

    def snap(self, sem):
        if self.mode == "dry":
            self._snaps.append(sem.n)
            return sem.n
        v = self._snaps[self._snap_pos]
        self._snap_pos += 1
        return v

    def run(self, blk, emit_fn):
        self.mode = "dry"
        self.cur = None
        self._snap_pos = 0
        emit_fn(self)
        sections = {"pe": blk.tensor, "ve": blk.vector, "sc": blk.scalar,
                    "sy": blk.sync, "gp": blk.gpsimd}
        for name in self.ENGS:
            def body(eng, name=name):
                self.mode = "real"
                self.cur = name
                self.real_eng = eng
                self.pos = {e: 0 for e in self.ENGS}
                self._snap_pos = 0
                emit_fn(self)
            sections[name](body)
        self.mode = "dry"
        self.cur = None



def build_program():
    nc = bass.Bass("TRN2", target_bir_lowering=False,
                   detect_race_conditions=False)

    xT = nc.dram_tensor("xT", [D, L], F32, kind="ExternalInput").ap()
    WfcT = nc.dram_tensor("WfcT", [D, HID], F32, kind="ExternalInput").ap()
    WpT = nc.dram_tensor("WpT", [HID, D], F32, kind="ExternalInput").ap()
    Wq = nc.dram_tensor("Wq", [D, D], F32, kind="ExternalInput").ap()
    Wkv = nc.dram_tensor("Wkv", [D, 2 * D], F32, kind="ExternalInput").ap()
    Wo = nc.dram_tensor("Wo", [D, D], F32, kind="ExternalInput").ap()
    bo_col = nc.dram_tensor("bo_col", [128, DC], F32, kind="ExternalInput").ap()
    lnw_col = nc.dram_tensor("lnw_col", [128, HC], F32, kind="ExternalInput").ap()
    lnb_col = nc.dram_tensor("lnb_col", [128, HC], F32, kind="ExternalInput").ap()
    med_bd_d = nc.dram_tensor("med_bd", [NP, 128, 128], F32, kind="ExternalInput").ap()
    ident_bf_d = nc.dram_tensor("ident_bf", [128, 128], BF16, kind="ExternalInput").ap()
    ident_f_d = nc.dram_tensor("ident_f", [128, 128], F32, kind="ExternalInput").ap()
    mask_bd_d = nc.dram_tensor("mask_bd", [128, 128], BF16, kind="ExternalInput").ap()
    ones_d = nc.dram_tensor("ones_misc", [128, 1], F32, kind="ExternalInput").ap()
    onesrow_d = nc.dram_tensor("ones_row", [1, 128], F32, kind="ExternalInput").ap()
    ind2_d = nc.dram_tensor("ind2", [2, 128], F32, kind="ExternalInput").ap()

    outT = nc.dram_tensor("outT", [D, L], F32, kind="ExternalOutput").ap()
    _dbg = os.environ.get("KMK_DEBUG") == "1"
    if _dbg:
        dbg_xn = nc.dram_tensor("dbg_xn", [128, DC * TT], BF16, kind="ExternalOutput").ap()
        dbg_g = nc.dram_tensor("dbg_g", [128, HC * TT], BF16, kind="ExternalOutput").ap()
        dbg_v = nc.dram_tensor("dbg_v", [4, TT], F32, kind="ExternalOutput").ap()
        dbg_c = nc.dram_tensor("dbg_c", [128, 3 * DC + 2], F32, kind="ExternalOutput").ap()
        dbg_w = nc.dram_tensor("dbg_w", [128, 3 * 1024], BF16, kind="ExternalOutput").ap()
        dbg_p0 = nc.dram_tensor("dbg_p0", [128, 8], F32, kind="ExternalOutput").ap()

    sc_fc = nc.dram_tensor("sc_fc", [HC, 128, DC, 128], BF16)
    sc_p = nc.dram_tensor("sc_p", [DC, 128, HC, 128], BF16)

    with ExitStack() as top:
        et = top.enter_context

        sems = {
            "sw": Sem(et(nc.semaphore("s_sw")), 16),
            "sq": Sem(et(nc.semaphore("s_sq")), 16),
            "sg": Sem(et(nc.semaphore("s_sg")), 16),
            "wi0": Sem(et(nc.semaphore("s_wi0")), 16),
            "wi1": Sem(et(nc.semaphore("s_wi1")), 16),
            "wi2": Sem(et(nc.semaphore("s_wi2")), 16),
            "wi3": Sem(et(nc.semaphore("s_wi3")), 16),
            "wp0": Sem(et(nc.semaphore("s_wp0")), 16),
            "wp1": Sem(et(nc.semaphore("s_wp1")), 16),
            "pe": Sem(et(nc.semaphore("s_pe")), 1),
            "gpc": Sem(et(nc.semaphore("s_gpc")), 1),
            "ve": Sem(et(nc.semaphore("s_ve")), 1),
            "sc": Sem(et(nc.semaphore("s_sc")), 1),
        }
        SW, SQ, SG = sems["sw"], sems["sq"], sems["sg"]
        PE, VE, SC = sems["pe"], sems["ve"], sems["sc"]
        WI = (sems["wi0"], sems["wi1"], sems["wi2"], sems["wi3"])
        GPC = sems["gpc"]
        WP = (sems["wp0"], sems["wp1"])

        x_sb = et(nc.sbuf_tensor("x_sb", [128, DC * L], F32))
        xn_bf = et(nc.sbuf_tensor("xn_bf", [128, DC * TT], BF16))
        xbf2 = et(nc.sbuf_tensor("xbf2", [128, 2 * TT], BF16))
        x2b = et(nc.sbuf_tensor("x2b", [128, TT], BF16))
        ones_bf = et(nc.sbuf_tensor("ones_bf", [128, 1], BF16))
        ones_f = et(nc.sbuf_tensor("ones_f", [128, 1], F32))
        onesrow = et(nc.sbuf_tensor("onesrow", [1, 128], F32))
        ind2 = et(nc.sbuf_tensor("ind2_sb", [2, 128], F32))
        ident_bf = et(nc.sbuf_tensor("ident_bf_sb", [128, 128], BF16))
        ident_f = et(nc.sbuf_tensor("ident_f_sb", [128, 128], F32))
        mask_bd = et(nc.sbuf_tensor("mask_bd_sb", [128, 128], BF16))
        lnw_c = et(nc.sbuf_tensor("lnw_c", [128, HC], F32))
        lnb_c = et(nc.sbuf_tensor("lnb_c", [128, HC], F32))
        bo_c = et(nc.sbuf_tensor("bo_c", [128, DC], F32))
        c1_c = et(nc.sbuf_tensor("c1_c", [128, DC], F32))
        c2h_c = et(nc.sbuf_tensor("c2h_c", [128, DC], F32))
        vec_a = et(nc.sbuf_tensor("vec_a", [1, TT], F32))
        vec_b = et(nc.sbuf_tensor("vec_b", [1, TT], F32))
        vec_c = et(nc.sbuf_tensor("vec_c", [1, TT], F32))
        thr_b = et(nc.sbuf_tensor("thr_b", [128, 2], F32))
        one1 = et(nc.sbuf_tensor("one1", [1, 1], F32))
        jnk = et(nc.sbuf_tensor("jnk", [128, 2], F32))
        eps1 = et(nc.sbuf_tensor("eps1", [1, 1], F32))

        psA = et(nc.psum_tensor("psA", [128, 512], F32))
        psB = et(nc.psum_tensor("psB", [128, 512], F32))
        psC = et(nc.psum_tensor("psC", [128, 512], F32))
        psD = et(nc.psum_tensor("psD", [128, 512], F32))
        psE = et(nc.psum_tensor("psE", [128, 512], F32))
        psF = et(nc.psum_tensor("psF", [128, 512], F32))
        psT1 = et(nc.psum_tensor("psT1", [128, 1024], BF16))
        psT2 = et(nc.psum_tensor("psT2", [128, 1024], BF16))

        def emit_stream_ln(em, t0, tw, scr, gp_apply=False, msb=None, rsb=None,
                           expln=False, castbuf=None):
            pe, ve, sc = em.e("pe"), em.e("ve"), em.e("sc")
            gp = em.e("gp")
            em.wait("ve", PE)
            for dc in range(DC):
                em.inc(ve.tensor_copy(castbuf[:, dc * tw:(dc + 1) * tw],
                                      x_sb[:, dc * L + t0: dc * L + t0 + tw]), VE)
            casts_done = em.snap(VE)
            em.wait("pe", VE, casts_done)
            for dc in range(DC):
                em.inc(pe.matmul(psE[0:1, 0:tw], ones_bf[:],
                                 castbuf[:, dc * tw:(dc + 1) * tw],
                                 start=(dc == 0), stop=(dc == DC - 1)), PE)
            sums_done = em.snap(PE)
            em.wait("ve", PE, sums_done)
            for dc in range(DC):
                cb = castbuf[:, dc * tw:(dc + 1) * tw]
                em.inc(ve.tensor_tensor(cb, cb, cb, op=ALU.mult), VE)
            sq_done = em.snap(VE)
            em.wait("pe", VE, sq_done)
            for dc in range(DC):
                em.inc(pe.matmul(psF[0:1, 0:tw], ones_bf[:],
                                 castbuf[:, dc * tw:(dc + 1) * tw],
                                 start=(dc == 0), stop=(dc == DC - 1)), PE)
            em.wait("ve", PE)
            em.inc(ve.tensor_scalar(vec_a[:, 0:tw], psE[0:1, 0:tw], 1.0 / D, None,
                                    op0=ALU.mult), VE)
            em.inc(ve.tensor_scalar(vec_b[:, 0:tw], psF[0:1, 0:tw], 1.0 / D, None,
                                    op0=ALU.mult), VE)
            em.inc(ve.tensor_tensor(vec_c[:, 0:tw], vec_a[:, 0:tw], vec_a[:, 0:tw],
                                    op=ALU.mult), VE)
            em.inc(ve.tensor_tensor(vec_b[:, 0:tw], vec_b[:, 0:tw], vec_c[:, 0:tw],
                                    op=ALU.subtract), VE)
            em.wait("sc", VE)
            if expln:
                em.inc(sc.activation(vec_b[:, 0:tw], vec_b[:, 0:tw], AF.Ln,
                                     bias=eps1[:], scale=1.0), SC)
                em.inc(sc.activation(vec_c[:, 0:tw], vec_b[:, 0:tw], AF.Exp,
                                     bias=0.0, scale=-0.5), SC)
                em.wait("pe", SC)
                em.wait("ve", SC)
            else:
                em.inc(sc.activation(vec_b[:, 0:tw], vec_b[:, 0:tw], AF.Sqrt,
                                     bias=eps1[:], scale=1.0), SC)
                em.wait("ve", SC)
                em.inc(ve.reciprocal(vec_c[:, 0:tw], vec_b[:, 0:tw]), VE)
                em.wait("pe", VE)
            em.inc(pe.matmul(psC[:, 0:tw], onesrow[:], vec_a[:, 0:tw],
                             start=True, stop=True), PE)
            em.inc(pe.matmul(psD[:, 0:tw], onesrow[:], vec_c[:, 0:tw],
                             start=True, stop=True), PE)
            if gp_apply:
                em.wait("sc", PE)
                em.inc(sc.activation(msb[:, 0:tw], psC[:, 0:tw], AF.Copy), SC)
                em.inc(sc.activation(rsb[:, 0:tw], psD[:, 0:tw], AF.Copy), SC)
                em.wait("ve", SC)
                for dc in range(DC):
                    xsl = x_sb[:, dc * L + t0: dc * L + t0 + tw]
                    em.inc(ve.tensor_tensor(msb[:, tw:2 * tw], xsl,
                                            msb[:, 0:tw],
                                            op=ALU.subtract), VE)
                    em.inc(ve.tensor_tensor(xn_bf[:, dc * TT: dc * TT + tw],
                                            msb[:, tw:2 * tw], rsb[:, 0:tw],
                                            op=ALU.mult), VE)
                em.inc(ve.tensor_copy(jnk[:, 0:1], jnk[:, 1:2]), GPC)
            else:
                em.wait("ve", PE)
                for dc in range(DC):
                    xsl = x_sb[:, dc * L + t0: dc * L + t0 + tw]
                    em.inc(ve.tensor_tensor(scr[:, 0:tw], xsl, psC[:, 0:tw],
                                            op=ALU.subtract), VE)
                    em.inc(ve.tensor_tensor(xn_bf[:, dc * TT: dc * TT + tw],
                                            scr[:, 0:tw], psD[:, 0:tw],
                                            op=ALU.mult), VE)

        def emit_ffn_ln(em, ti, scr):
            pe, ve, sc = em.e("pe"), em.e("ve"), em.e("sc")
            t0 = ti * TT
            tw = TT
            pT1 = psT1[:].bitcast(F32)
            pT2 = psT2[:].bitcast(F32)
            em.wait("ve", PE)
            for dc in range(DC):
                em.inc(ve.tensor_copy(xn_bf[:, dc * TT:dc * TT + tw],
                                      x_sb[:, dc * L + t0: dc * L + t0 + tw]), VE)
            casts_done = em.snap(VE)
            em.wait("pe", VE, casts_done)
            for dc in range(DC):
                em.inc(pe.matmul(psE[0:1, 0:tw], ones_bf[:],
                                 xn_bf[:, dc * TT:dc * TT + tw],
                                 start=(dc == 0), stop=(dc == DC - 1)), PE)
            sums_done = em.snap(PE)
            em.wait("sc", PE, sums_done)
            for dc in range(DC):
                cb = xn_bf[:, dc * TT:dc * TT + tw]
                em.inc(sc.activation(cb, cb, AF.Square), SC)
            sq_done = em.snap(SC)
            em.wait("pe", SC, sq_done)
            for dc in range(DC):
                em.inc(pe.matmul(psF[0:1, 0:tw], ones_bf[:],
                                 xn_bf[:, dc * TT:dc * TT + tw],
                                 start=(dc == 0), stop=(dc == DC - 1)), PE)
            lnstats_done = em.snap(PE)
            em.wait("ve", PE, lnstats_done)
            em.inc(ve.tensor_scalar(vec_a[:], psE[0:1, :], 1.0 / D, None,
                                    op0=ALU.mult), VE)
            em.inc(ve.tensor_scalar(vec_b[:], psF[0:1, :], 1.0 / D, None,
                                    op0=ALU.mult), VE)
            em.inc(ve.tensor_tensor(vec_c[:], vec_a[:], vec_a[:],
                                    op=ALU.mult), VE)
            em.inc(ve.tensor_tensor(vec_b[:], vec_b[:], vec_c[:],
                                    op=ALU.subtract), VE)
            em.wait("sc", VE)
            em.inc(sc.activation(vec_b[:], vec_b[:], AF.Sqrt,
                                 bias=eps1[:], scale=1.0), SC)
            em.wait("ve", SC)
            em.inc(ve.reciprocal(vec_c[:], vec_b[:]), VE)
            em.wait("pe", VE)
            em.inc(pe.matmul(pT1[:, 0:tw], onesrow[:], vec_a[:],
                             start=True, stop=True), PE)
            em.inc(pe.matmul(pT2[:, 0:tw], onesrow[:], vec_c[:],
                             start=True, stop=True), PE)
            bca = em.snap(PE)
            em.wait("ve", PE, bca)
            for dc in range(DC):
                xsl = x_sb[:, dc * L + t0: dc * L + t0 + tw]
                em.inc(ve.tensor_tensor(scr[:, 0:tw], xsl, pT1[:, 0:tw],
                                        op=ALU.subtract), VE)
                em.inc(ve.tensor_tensor(xn_bf[:, dc * TT: dc * TT + tw],
                                        scr[:, 0:tw], pT2[:, 0:tw],
                                        op=ALU.mult), VE)

        with nc.sbuf_tensor("p0_raw", [128, 4096], F32) as raw, \
             nc.sbuf_tensor("p0_raw2", [128, 4096], F32) as raw2, \
             nc.sbuf_tensor("p0_t1", [128, 2048], F32) as t1, \
             nc.sbuf_tensor("p0_t2", [128, 2048], F32) as t2, \
             nc.sbuf_tensor("p0_q", [128, 2048], BF16) as qb, \
             nc.sbuf_tensor("p0_acc", [128, 34], F32) as acc, \
             nc.sbuf_tensor("p0_cacc", [128, 16], F32) as cacc, \
             nc.Block() as blk:

            def emit(em):
                pe, ve, sc, sy, gp = (em.e("pe"), em.e("ve"), em.e("sc"),
                                      em.e("sy"), em.e("gp"))
                for dc in range(DC):
                    em.inc(gp.dma_start(out=x_sb[:, dc * L:(dc + 1) * L],
                                        in_=xT[dc * 128:(dc + 1) * 128, :]), SG)
                em.inc(sy.dma_start(out=ident_bf[:], in_=ident_bf_d), SW)
                em.inc(sy.dma_start(out=ident_f[:], in_=ident_f_d), SW)
                em.inc(sy.dma_start(out=mask_bd[:], in_=mask_bd_d), SW)
                em.inc(sy.dma_start(out=onesrow[:], in_=onesrow_d), SW)
                em.inc(sy.dma_start(out=ind2[:], in_=ind2_d), SW)
                em.inc(sy.dma_start(out=lnw_c[:], in_=lnw_col), SW)
                em.inc(sy.dma_start(out=lnb_c[:], in_=lnb_col), SW)
                em.inc(sy.dma_start(out=bo_c[:], in_=bo_col), SW)
                em.inc(sy.dma_start(out=ones_f[:], in_=ones_d), SW)
                med_done = em.snap(SW)
                em.wait("ve", SW, med_done)
                em.inc(ve.tensor_copy(ones_bf[:], ones_f[:]), VE)
                em.inc(ve.memset(cacc[:], 0.0), VE)
                em.inc(ve.memset(eps1[:], LNEPS), VE)
                em.inc(ve.memset(jnk[:], 0.0), VE)

                em.inc(ve.memset(acc[:, 0:1], 0.0), VE)
                em.inc(ve.memset(acc[:, 1:2], 0.0), VE)
                for dc in range(DC):
                    rb = raw if dc % 2 == 0 else raw2
                    if dc >= 2:
                        em.wait("sy", SC)
                        em.wait("sy", VE)
                    em.inc(sy.dma_start(out=rb[:],
                                        in_=WfcT[dc * 128:(dc + 1) * 128, :]), SW)
                    em.wait("sc", SW)
                    em.inc(sc.activation(t1[:], rb[:, 0:2048], AF.Abs,
                                         accum_out=acc[:, 2:3]), SC)
                    em.inc(sc.activation(t2[:], rb[:, 2048:4096], AF.Abs,
                                         accum_out=acc[:, 3:4]), SC)
                    em.wait("ve", SC)
                    em.inc(ve.tensor_tensor(acc[:, 0:1], acc[:, 0:1], acc[:, 2:3],
                                            op=ALU.add), VE)
                    em.inc(ve.tensor_tensor(acc[:, 1:2], acc[:, 1:2], acc[:, 3:4],
                                            op=ALU.add), VE)
                em.inc(ve.tensor_copy(jnk[:, 0:1], jnk[:, 1:2]), VE)
                em.inc(ve.tensor_tensor(acc[:, 4:5], acc[:, 0:1], acc[:, 1:2],
                                        op=ALU.add), VE)
                em.wait("pe", VE)
                em.inc(pe.matmul(psB[0:1, 0:1], ones_f[:], acc[:, 4:5],
                                 start=True, stop=True), PE)
                em.wait("ve", PE)
                em.inc(ve.tensor_scalar(one1[:], psB[0:1, 0:1], 0.5 / (D * HID),
                                        0.5 * QEPS, op0=ALU.mult, op1=ALU.add), VE)
                em.wait("pe", VE)
                em.inc(pe.matmul(psB[:, 0:1], onesrow[:], one1[:],
                                 start=True, stop=True), PE)
                em.wait("ve", PE)
                em.inc(ve.tensor_copy(thr_b[:, 0:1], psB[:, 0:1]), VE)
                em.inc(ve.tensor_scalar(thr_b[:, 1:2], psB[:, 0:1], -1.0, None,
                                        op0=ALU.mult), VE)
                thrF = em.snap(VE)

                for dc in range(DC):
                    for h in range(2):
                        k = dc * 2 + h
                        rb = raw if k % 2 == 0 else raw2
                        if k == 0:
                            em.wait("sy", VE, thrF)
                        elif k >= 2:
                            em.wait("sy", VE)
                        em.inc(sy.dma_start(
                            out=rb[:, 0:2048],
                            in_=WfcT[dc * 128:(dc + 1) * 128,
                                     h * 2048:(h + 1) * 2048]), SW)
                        em.wait("ve", SW)
                        em.inc(ve.tensor_scalar(t1[:], rb[:, 0:2048],
                                                thr_b[:, 0:1], None,
                                                op0=ALU.is_gt), VE)
                        em.inc(ve.tensor_scalar(t2[:], rb[:, 0:2048],
                                                thr_b[:, 1:2], None,
                                                op0=ALU.is_lt), VE)
                        em.wait("ve", SG)
                        em.inc(ve.tensor_tensor(qb[:], t1[:], t2[:],
                                                op=ALU.subtract), VE)
                        em.wait("gp", VE)
                        dst = sc_fc.ap()[h * 16:(h + 1) * 16, :, dc, :]
                        dst = dst.transpose([1, 0, 2])
                        em.inc(gp.dma_start(
                            out=dst,
                            in_=qb[:].rearrange("p (a b) -> p a b", b=128)), SG)

                em.inc(ve.memset(acc[:, 0:1], 0.0), VE)
                em.inc(ve.memset(acc[:, 1:2], 0.0), VE)
                for hc in range(HC):
                    rb = raw if hc % 2 == 0 else raw2
                    em.wait("sy", SC)
                    em.wait("sy", VE)
                    em.inc(sy.dma_start(out=rb[:, 0:1024],
                                        in_=WpT[hc * 128:(hc + 1) * 128, :]), SW)
                    em.wait("sc", SW)
                    em.inc(sc.activation(t1[:, 0:1024], rb[:, 0:1024], AF.Abs,
                                         accum_out=acc[:, 2 + hc % 2:3 + hc % 2]),
                           SC)
                    em.wait("ve", SC)
                    em.inc(ve.tensor_tensor(acc[:, hc % 2:1 + hc % 2],
                                            acc[:, hc % 2:1 + hc % 2],
                                            acc[:, 2 + hc % 2:3 + hc % 2],
                                            op=ALU.add), VE)
                em.inc(ve.tensor_copy(jnk[:, 0:1], jnk[:, 1:2]), VE)
                em.inc(ve.tensor_tensor(acc[:, 4:5], acc[:, 0:1], acc[:, 1:2],
                                        op=ALU.add), VE)
                em.wait("pe", VE)
                em.inc(pe.matmul(psB[0:1, 0:1], ones_f[:], acc[:, 4:5],
                                 start=True, stop=True), PE)
                em.wait("ve", PE)
                em.inc(ve.tensor_scalar(one1[:], psB[0:1, 0:1], 0.5 / (D * HID),
                                        0.5 * QEPS, op0=ALU.mult, op1=ALU.add), VE)
                em.wait("pe", VE)
                em.inc(pe.matmul(psB[:, 0:1], onesrow[:], one1[:],
                                 start=True, stop=True), PE)
                em.wait("ve", PE)
                em.inc(ve.tensor_copy(thr_b[:, 0:1], psB[:, 0:1]), VE)
                em.inc(ve.tensor_scalar(thr_b[:, 1:2], psB[:, 0:1], -1.0, None,
                                        op0=ALU.mult), VE)
                thrP = em.snap(VE)

                for hc in range(HC):
                    rb = raw if hc % 2 == 0 else raw2
                    if hc == 0:
                        em.wait("sy", VE, thrP)
                    elif hc >= 2:
                        em.wait("sy", VE)
                    em.inc(sy.dma_start(out=rb[:, 0:1024],
                                        in_=WpT[hc * 128:(hc + 1) * 128, :]), SW)
                    em.wait("ve", SW)
                    em.wait("ve", PE)
                    em.inc(ve.tensor_scalar(t1[:, 0:1024], rb[:, 0:1024],
                                            thr_b[:, 0:1], None, op0=ALU.is_gt),
                           VE)
                    em.inc(ve.tensor_scalar(t2[:, 0:1024], rb[:, 0:1024],
                                            thr_b[:, 1:2], None, op0=ALU.is_lt),
                           VE)
                    em.inc(ve.tensor_tensor(t1[:, 0:1024], t1[:, 0:1024],
                                            t2[:, 0:1024], op=ALU.subtract), VE)
                    em.wait("ve", SG)
                    em.inc(ve.tensor_scalar(qb[:, 0:1024], t1[:, 0:1024],
                                            lnw_c[:, hc:hc + 1], None,
                                            op0=ALU.mult), VE)
                    ve_done = em.snap(VE)
                    em.wait("pe", VE, ve_done)
                    for dcq in range(DC):
                        em.inc(pe.matmul(psE[:, dcq:dcq + 1],
                                         t1[:, dcq * 128:(dcq + 1) * 128],
                                         lnb_c[:, hc:hc + 1],
                                         start=True, stop=True,
                                         skip_group_check=True), PE)
                    for dcq in range(DC):
                        em.inc(pe.matmul(psF[:, dcq:dcq + 1],
                                         qb[:, dcq * 128:(dcq + 1) * 128],
                                         ones_bf[:],
                                         start=True, stop=True,
                                         skip_group_check=True), PE)
                    em.wait("ve", PE)
                    em.inc(ve.tensor_tensor(cacc[:, 0:8], cacc[:, 0:8],
                                            psE[:, 0:8], op=ALU.add), VE)
                    em.inc(ve.tensor_tensor(cacc[:, 8:16], cacc[:, 8:16],
                                            psF[:, 0:8], op=ALU.add), VE)
                    em.wait("gp", VE)
                    dst = sc_p.ap()[:, :, hc, :].transpose([1, 0, 2])
                    em.inc(gp.dma_start(
                        out=dst,
                        in_=qb[:, 0:1024].rearrange("p (a b) -> p a b", b=128)), SG)
                em.inc(ve.tensor_scalar(c2h_c[:], cacc[:, 0:8], 0.5, None,
                                        op0=ALU.mult), VE)
                em.inc(ve.tensor_copy(c1_c[:], cacc[:, 8:16]), VE)
                if _dbg:
                    em.wait("gp", VE)
                    em.inc(em.e("gp").dma_start(out=dbg_p0, in_=acc[:, 0:8]), SG)
                em.wait("gp", SG)
                em.wait("sy", SW)

            Em(sems).run(blk, emit)

        def ffn_phase(tag):
            with nc.sbuf_tensor(f"{tag}_g", [128, HC * TT], BF16) as g_sb, \
                 nc.sbuf_tensor(f"{tag}_wi", [128, 3 * 1024], BF16) as wfc_b, \
                 nc.sbuf_tensor(f"{tag}_wp", [128, 2 * 4096], BF16) as wp_b, \
                 nc.sbuf_tensor(f"{tag}_scr", [128, 512], F32) as scr, \
                 nc.sbuf_tensor(f"{tag}_ub", [128, 512], F32) as ub_sb, \
                 nc.Block() as blk:

                def emit(em):
                    pe, ve, sc, sy = em.e("pe"), em.e("ve"), em.e("sc"), em.e("sy")
                    in_dma = {}
                    pj_dma = {}
                    pe_grp = {}
                    pe_pj = {}
                    ve_epi = {}
                    nwi = [0]
                    npj = [0]

                    def issue_wi():
                        k = nwi[0]
                        if k >= NT * HC:
                            return
                        nwi[0] += 1
                        if k >= 3:
                            em.wait("sy", PE, pe_grp.get(k - 3, 0))
                        wsem = WI[k % 3]
                        hc = k % HC
                        em.inc(sy.dma_start(
                            out=wfc_b[:, (k % 3) * 1024:(k % 3) * 1024 + 1024],
                            in_=sc_fc.ap()[hc].rearrange("p a b -> p (a b)")),
                            wsem)
                        in_dma[k] = wsem.n if em.mode == "dry" else 0

                    def issue_pj():
                        k = npj[0]
                        if k >= NT * DC:
                            return
                        npj[0] += 1
                        if k >= 2:
                            em.wait("sc", PE, pe_pj.get(k - 2, 0))
                        wsem = WP[k % 2]
                        dc = k % DC
                        em.inc(sc.dma_start(
                            out=wp_b[:, (k % 2) * 4096:(k % 2) * 4096 + 4096],
                            in_=sc_p.ap()[dc].rearrange("p a b -> p (a b)")),
                            wsem)
                        pj_dma[k] = wsem.n if em.mode == "dry" else 0

                    for _ in range(3):
                        issue_wi()
                    for _ in range(2):
                        issue_pj()

                    emit_ffn_ln(em, 0, scr)
                    for ti in range(NT):
                        t0 = ti * TT
                        sc_evac = {}
                        pe_stats = {}
                        for hc in range(HC + 1):
                            if hc < HC:
                                k = ti * HC + hc
                                em.wait("pe", WI[k % 3], in_dma.get(k, 0))
                                if hc == 0:
                                    em.wait("pe", VE)
                                if hc >= 2:
                                    em.wait("pe", SC, sc_evac.get(hc - 2, 0))
                                pb = psA if hc % 2 == 0 else psB
                                wsl = wfc_b[:, (k % 3) * 1024:(k % 3) * 1024 + 1024]
                                for dc in range(DC):
                                    em.inc(pe.matmul(
                                        pb[:], wsl[:, dc * 128:(dc + 1) * 128],
                                        xn_bf[:, dc * TT:(dc + 1) * TT],
                                        start=(dc == 0), stop=(dc == DC - 1)), PE)
                                pe_grp[k] = PE.n if em.mode == "dry" else 0
                                issue_wi()
                            if hc >= 1:
                                hp = hc - 1
                                em.wait("pe", SC, sc_evac.get(hp, 0))
                                em.inc(pe.matmul(psE[0:1, :], ones_bf[:],
                                                 g_sb[:, hp * TT:(hp + 1) * TT],
                                                 start=(hp == 0),
                                                 stop=(hp == HC - 1)), PE)
                                em.inc(pe.matmul(psF[0:1, :], ones_bf[:], x2b[:],
                                                 start=(hp == 0),
                                                 stop=(hp == HC - 1)), PE)
                                pe_stats[hp] = PE.n if em.mode == "dry" else 0
                            if hc < HC:
                                em.wait("sc", PE, pe_grp.get(k, 0))
                                em.inc(sc.activation(
                                    g_sb[:, hc * TT:(hc + 1) * TT],
                                    pb[:], AF.Gelu), SC)
                                if hc >= 1:
                                    em.wait("sc", PE, pe_stats.get(hc - 1, 0))
                                em.inc(sc.activation(
                                    x2b[:],
                                    g_sb[:, hc * TT:(hc + 1) * TT],
                                    AF.Square), SC)
                                sc_evac[hc] = SC.n if em.mode == "dry" else 0
                        stats_done = em.snap(PE)
                        em.wait("ve", PE, stats_done)
                        em.inc(ve.tensor_scalar(vec_a[:], psE[0:1, :],
                                                1.0 / HID, None,
                                                op0=ALU.mult), VE)
                        em.inc(ve.tensor_scalar(vec_b[:], psF[0:1, :],
                                                1.0 / HID, None,
                                                op0=ALU.mult), VE)
                        em.inc(ve.tensor_tensor(vec_c[:], vec_a[:],
                                                vec_a[:], op=ALU.mult), VE)
                        em.inc(ve.tensor_tensor(vec_b[:], vec_b[:],
                                                vec_c[:],
                                                op=ALU.subtract), VE)
                        em.wait("sc", VE)
                        em.inc(sc.activation(vec_b[:], vec_b[:], AF.Sqrt,
                                             bias=eps1[:], scale=1.0), SC)
                        em.wait("ve", SC)
                        em.inc(ve.reciprocal(vec_c[:], vec_b[:]), VE)
                        em.inc(ve.tensor_scalar(vec_c[:], vec_c[:], 0.5,
                                                None, op0=ALU.mult), VE)
                        em.inc(ve.tensor_tensor(vec_b[:], vec_c[:],
                                                vec_a[:], op=ALU.mult), VE)
                        em.inc(ve.tensor_scalar(vec_b[:], vec_b[:], -1.0,
                                                None, op0=ALU.mult), VE)
                        hfin_done = em.snap(VE)
                        em.wait("pe", VE, hfin_done)
                        em.inc(pe.matmul(psC[:], onesrow[:], vec_c[:],
                                         start=True, stop=True), PE)
                        em.inc(pe.matmul(psD[:], onesrow[:], vec_b[:],
                                         start=True, stop=True), PE)
                        em.wait("ve", PE)
                        em.inc(ve.tensor_copy(ub_sb[:], psC[:]), VE)
                        if ti + 1 < NT:
                            emit_ffn_ln(em, ti + 1, scr)
                        for dc in range(DC):
                            k = ti * DC + dc
                            em.wait("pe", WP[k % 2], pj_dma.get(k, 0))
                            if dc >= 2:
                                em.wait("pe", VE, ve_epi.get(dc - 2, 0))
                            pb = psA if dc % 2 == 0 else psB
                            wsl = wp_b[:, (k % 2) * 4096:(k % 2) * 4096 + 4096]
                            for hc in range(HC):
                                em.inc(pe.matmul(
                                    pb[:], wsl[:, hc * 128:(hc + 1) * 128],
                                    g_sb[:, hc * TT:(hc + 1) * TT],
                                    start=(hc == 0), stop=(hc == HC - 1)), PE)
                            pe_pj[k] = PE.n if em.mode == "dry" else 0
                            issue_pj()
                            xsl = x_sb[:, dc * L + t0: dc * L + t0 + TT]
                            em.wait("ve", PE)
                            em.inc(ve.tensor_tensor(scr[:], pb[:], ub_sb[:],
                                                    op=ALU.mult), VE)
                            em.inc(ve.tensor_tensor(xsl, xsl, scr[:],
                                                    op=ALU.add), VE)
                            em.wait("sc", VE)
                            em.inc(sc.activation(scr[:], psD[:],
                                                 AF.Identity,
                                                 bias=c2h_c[:, dc:dc + 1],
                                                 scale=c1_c[:, dc:dc + 1]), SC)
                            em.wait("ve", SC)
                            em.inc(ve.tensor_tensor(xsl, xsl, scr[:],
                                                    op=ALU.add), VE)
                            ve_epi[dc] = VE.n if em.mode == "dry" else 0

                    if _dbg and tag == "f1":
                        gp = em.e("gp")
                        em.wait("gp", VE)
                        em.wait("gp", SC)
                        em.inc(gp.dma_start(out=dbg_xn, in_=xn_bf[:]), SG)
                        em.inc(gp.dma_start(out=dbg_g, in_=g_sb[:]), SG)
                        em.inc(gp.dma_start(out=dbg_v[0:1, :], in_=vec_a[:]), SG)
                        em.inc(gp.dma_start(out=dbg_v[1:2, :], in_=vec_b[:]), SG)
                        em.inc(gp.dma_start(out=dbg_v[2:3, :], in_=vec_c[:]), SG)
                        em.inc(gp.dma_start(out=dbg_v[3:4, :], in_=ub_sb[0:1, :]), SG)
                        em.inc(gp.dma_start(out=dbg_c[:, 0:DC], in_=c1_c[:]), SG)
                        em.inc(gp.dma_start(out=dbg_c[:, DC:2 * DC], in_=c2h_c[:]), SG)
                        em.inc(gp.dma_start(out=dbg_c[:, 2 * DC:3 * DC], in_=bo_c[:]), SG)
                        em.inc(gp.dma_start(out=dbg_c[:, 3 * DC:3 * DC + 2], in_=thr_b[:]), SG)
                        em.inc(gp.dma_start(out=dbg_w[:, 0:1024], in_=wfc_b[:, 0:1024]), SG)
                        em.inc(gp.dma_start(out=xn_bf[:, 0:1024],
                                            in_=sc_fc.ap()[0].rearrange("p a b -> p (a b)")), SG)
                        em.wait("gp", SG)
                        em.inc(gp.dma_start(out=dbg_w[:, 1024:2048], in_=xn_bf[:, 0:1024]), SG)
                        em.wait("gp", SG)

                Em(sems).run(blk, emit)

        _ph = os.environ.get("KMK_PHASES", "full")
        if _ph != "0":
            ffn_phase("f1")

        def attn_a_phase():
          with nc.sbuf_tensor("a_wkv", [128, DC * 2048], BF16) as wkv_b, \
             nc.sbuf_tensor("a_k", [128, NP * AB], BF16) as k_b, \
             nc.sbuf_tensor("a_v", [128, (AB // 128) * 1024], BF16) as v_b, \
             nc.sbuf_tensor("a_es", [128, NP * AB], BF16) as es_b, \
             nc.sbuf_tensor("a_est", [128, AB], BF16) as est_b, \
             nc.sbuf_tensor("a_mkv", [128, NP * 128], F32) as mkv_sb, \
             nc.sbuf_tensor("a_stat", [128, 56], F32) as ast, \
             nc.sbuf_tensor("a_scr", [128, 1024], F32) as scr, \
             nc.Block() as blk:

            def emit(em):
                pe, ve, sc, sy = em.e("pe"), em.e("ve"), em.e("sc"), em.e("sy")
                for p in range(NP):
                    if p > 0:
                        em.wait("sy", SC)
                    em.inc(sy.dma_start(out=scr[:, 0:128].bitcast(F32)
                                        if False else scr[:, 0:128],
                                        in_=med_bd_d[p]), SW)
                    em.wait("sc", SW)
                    em.inc(sc.activation(med_bf[:, p * 128:(p + 1) * 128],
                                         scr[:, 0:128],
                                         AF.Copy, scale=0.125), SC)
                wkv_cp = {}
                for k in range(DC * 2):
                    dic, h = k // 2, k % 2
                    slot = k % 2
                    bounce = scr[:, 0:1024] if slot == 0 else mkv_sb[:]
                    qeng = "sy" if slot == 0 else "sc"
                    eng = sy if slot == 0 else sc
                    qsem = SW if slot == 0 else WI[1]
                    if k >= 2:
                        em.wait(qeng, VE, wkv_cp[k - 2])
                    if k == 0:
                        em.wait("sy", SC)
                    em.inc(eng.dma_start(
                        out=bounce,
                        in_=Wkv[dic * 128:(dic + 1) * 128,
                                h * 1024:(h + 1) * 1024]), qsem)
                    em.wait("ve", qsem)
                    em.inc(ve.tensor_copy(
                        wkv_b[:, dic * 2048 + h * 1024:
                              dic * 2048 + h * 1024 + 1024],
                        bounce), VE)
                    wkv_cp[k] = VE.n if em.mode == "dry" else 0
                em.inc(ve.memset(ast[:, 0:8], -1.0e30), VE)
                em.inc(ve.memset(ast[:, 8:16], 0.0), VE)
                em.inc(ve.memset(mkv_sb[:], 0.0), VE)
                wkv_done = em.snap(VE)
                em.wait("pe", VE, wkv_done)

                sc_exp = {}

                KVB = (psA, psB, psC, psD)

                def emit_kv(b):
                    for p in range(NP):
                        pb = KVB[p % 4]
                        if p == 2 or p >= 4:
                            em.wait("pe", SC)
                        for dic in range(DC):
                            em.inc(pe.matmul(
                                pb[:, 0:AB],
                                wkv_b[:, dic * 2048 + p * 128:
                                      dic * 2048 + p * 128 + 128],
                                xn_bf[:, dic * TT:dic * TT + AB],
                                start=(dic == 0), stop=(dic == DC - 1)), PE)
                        em.wait("sc", PE)
                        em.inc(sc.activation(k_b[:, p * AB:(p + 1) * AB],
                                             pb[:, 0:AB], AF.Copy), SC)
                    for lc in range(AB // 128):
                        for hf in range(2):
                            i = lc * 2 + hf
                            pb = KVB[i % 4]
                            em.wait("pe", SC)
                            for dic in range(DC):
                                em.inc(pe.matmul(
                                    pb[:],
                                    xn_bf[:, dic * TT + lc * 128:
                                          dic * TT + lc * 128 + 128],
                                    wkv_b[:, dic * 2048 + 1024 + hf * 512:
                                          dic * 2048 + 1024 + hf * 512 + 512],
                                    start=(dic == 0), stop=(dic == DC - 1)), PE)
                            em.wait("sc", PE)
                            em.inc(sc.activation(
                                v_b[:, lc * 1024 + hf * 512:
                                    lc * 1024 + hf * 512 + 512],
                                pb[:], AF.Copy), SC)

                def emit_sk_softmax(b):
                    SKB = (psA, psB, psC, psD)
                    em.wait("pe", SC)
                    em.wait("pe", VE)
                    for p in range(NP):
                        pb = SKB[p % 4]
                        if p >= 4:
                            em.wait("pe", SC, sc_exp.get((b, p - 4), 0))
                        em.inc(pe.matmul(pb[:, 0:AB],
                                         med_bf[:, p * 128:(p + 1) * 128],
                                         k_b[:, p * AB:(p + 1) * AB],
                                         start=True, stop=True), PE)
                        sk_done = em.snap(PE)
                        em.wait("ve", PE, sk_done)
                        em.inc(ve.reduce_max(ast[:, 48 + p:49 + p], pb[:, 0:AB],
                                             axis=AX.X), VE)
                        em.inc(ve.tensor_copy(ast[:, 16 + p:17 + p],
                                              ast[:, p:p + 1]), VE)
                        em.inc(ve.tensor_copy(jnk[:, 0:1], jnk[:, 1:2]), VE)
                        em.inc(ve.tensor_scalar(ast[:, 24 + p:25 + p],
                                                ast[:, 48 + p:49 + p],
                                                ast[:, 16 + p:17 + p], -1.0,
                                                op0=ALU.max, op1=ALU.mult), VE)
                        em.inc(ve.tensor_scalar(ast[:, p:p + 1],
                                                ast[:, 48 + p:49 + p],
                                                ast[:, 16 + p:17 + p], None,
                                                op0=ALU.max), VE)
                        em.wait("sc", VE)
                        em.inc(sc.activation(ast[:, 32 + p:33 + p],
                                             ast[:, 16 + p:17 + p], AF.Exp,
                                             bias=ast[:, 24 + p:25 + p],
                                             scale=1.0), SC)
                        em.inc(sc.activation(es_b[:, p * AB:(p + 1) * AB],
                                             pb[:, 0:AB], AF.Exp,
                                             bias=ast[:, 24 + p:25 + p],
                                             scale=1.0,
                                             accum_out=ast[:, 40 + p:41 + p]), SC)
                        sc_exp[(b, p)] = SC.n if em.mode == "dry" else 0
                    em.wait("ve", SC)
                    for p in range(NP):
                        em.inc(ve.scalar_tensor_tensor(
                            ast[:, 8 + p:9 + p], ast[:, 8 + p:9 + p],
                            ast[:, 32 + p:33 + p], ast[:, 40 + p:41 + p],
                            op0=ALU.mult, op1=ALU.add), VE)

                def emit_mixer_tail(b):
                    for p in range(NP):
                        em.wait("pe", SC, sc_exp.get((b, p), 0))
                        em.wait("pe", VE)
                        for i in range(AB // 128):
                            pt = psT1 if i % 2 == 0 else psT2
                            em.inc(pe.transpose(
                                pt[:, 0:128],
                                es_b[:, p * AB + i * 128:p * AB + (i + 1) * 128],
                                ident_bf[:]), PE)
                            em.wait("ve", PE)
                            em.inc(ve.tensor_copy(est_b[:, i * 128:(i + 1) * 128],
                                                  pt[:, 0:128]), VE)
                        em.wait("pe", VE)
                        pp = psE if p % 2 == 0 else psF
                        for lc in range(AB // 128):
                            em.inc(pe.matmul(
                                pp[:, 0:128],
                                est_b[:, lc * 128:(lc + 1) * 128],
                                v_b[:, lc * 1024 + p * 128:
                                    lc * 1024 + p * 128 + 128],
                                start=(lc == 0), stop=(lc == AB // 128 - 1)), PE)
                        em.wait("ve", PE)
                        em.inc(ve.scalar_tensor_tensor(
                            mkv_sb[:, p * 128:(p + 1) * 128],
                            mkv_sb[:, p * 128:(p + 1) * 128],
                            ast[:, 32 + p:33 + p], pp[:, 0:128],
                            op0=ALU.mult, op1=ALU.add), VE)

                for b in range(NAB):
                    t0 = b * AB
                    if b >= 1:
                        emit_mixer_tail(b - 1)
                    emit_stream_ln(em, t0, AB, scr, gp_apply=True,
                                   msb=scr[:, 256:768], rsb=scr[:, 768:1024],
                                   expln=True, castbuf=es_b[:, 0:DC * AB])
                    em.wait("pe", GPC)
                    emit_kv(b)
                    emit_sk_softmax(b)
                emit_mixer_tail(NAB - 1)

                for p in range(NP):
                    em.inc(ve.reciprocal(ast[:, 16 + p:17 + p],
                                         ast[:, 8 + p:9 + p]), VE)
                for p in range(NP):
                    em.inc(ve.tensor_scalar(mkv_sb[:, p * 128:(p + 1) * 128],
                                            mkv_sb[:, p * 128:(p + 1) * 128],
                                            ast[:, 16 + p:17 + p], None,
                                            op0=ALU.mult), VE)
                for p in range(NP):
                    em.inc(ve.tensor_tensor(mkv_bf[:, p * 128:(p + 1) * 128],
                                            mkv_sb[:, p * 128:(p + 1) * 128],
                                            mask_bd[:], op=ALU.mult), VE)

            Em(sems).run(blk, emit)

        def attn_b_phase():
          with nc.sbuf_tensor("b_wq", [128, DC * 1024], BF16) as wq_b, \
             nc.sbuf_tensor("b_wo", [128, DC * 1024], BF16) as wo_b, \
             nc.sbuf_tensor("b_ao", [128, NP * TT], BF16) as ao_b, \
             nc.sbuf_tensor("b_q", [128, 2 * TT], BF16) as q_b, \
             nc.sbuf_tensor("b_qm", [128, 8 * 128], BF16) as qm_sb, \
             nc.sbuf_tensor("b_qmt", [128, 2 * TT], BF16) as qmT, \
             nc.sbuf_tensor("b_st", [128, 48], F32) as bst, \
             nc.sbuf_tensor("b_gscr", [128, 2048], F32) as gscr, \
             nc.Block() as blk:

            def emit(em):
                pe, ve, sc, sy = em.e("pe"), em.e("ve"), em.e("sc"), em.e("sy")
                wqo_cp = {}
                for k in range(DC * 2):
                    dic = k // 2
                    isq = (k % 2 == 0)
                    slot = k % 2
                    bounce = gscr[:, slot * 1024:slot * 1024 + 1024]
                    qeng = "sy" if slot == 0 else "sc"
                    eng = sy if slot == 0 else sc
                    qsem = SW if slot == 0 else WI[1]
                    if k >= 2:
                        em.wait(qeng, VE, wqo_cp[k - 2])
                    src = Wq if isq else Wo
                    wdst = wq_b if isq else wo_b
                    em.inc(eng.dma_start(
                        out=bounce,
                        in_=src[dic * 128:(dic + 1) * 128, :]), qsem)
                    em.wait("ve", qsem)
                    em.inc(ve.tensor_copy(wdst[:, dic * 1024:(dic + 1) * 1024],
                                          bounce), VE)
                    wqo_cp[k] = VE.n if em.mode == "dry" else 0
                w_done = em.snap(VE)
                em.wait("pe", VE, w_done)

                for ti in range(NT):
                    t0 = ti * TT
                    emit_stream_ln(em, t0, TT, gscr[:, 0:512], gp_apply=True,
                                   msb=gscr[:, 512:1536], rsb=gscr[:, 1536:2048],
                                   expln=True, castbuf=ao_b[:, 0:DC * TT])

                    def emit_qmm_pe(p):
                        pb = psA if p % 2 == 0 else psB
                        em.wait("pe", SC)
                        if p == 0:
                            em.wait("pe", GPC)
                            em.wait("pe", VE)
                        for dic in range(DC):
                            em.inc(pe.matmul(
                                pb[:],
                                wq_b[:, dic * 1024 + p * 128:
                                     dic * 1024 + p * 128 + 128],
                                xn_bf[:, dic * TT:(dic + 1) * TT],
                                start=(dic == 0), stop=(dic == DC - 1)), PE)
                        return em.snap(PE)

                    def emit_qmm_sc(p, mmdone):
                        pb = psA if p % 2 == 0 else psB
                        em.wait("sc", PE, mmdone)
                        em.inc(sc.activation(q_b[:, (p % 2) * TT:(p % 2) * TT + TT],
                                             pb[:], AF.Copy), SC)
                        return em.snap(SC)

                    qevacs = {}
                    qmmdone = {}
                    ao_done = {}
                    pet_done = {}
                    for p in range(NP + 1):
                        if p < NP:
                            qmmdone[p] = emit_qmm_pe(p)
                            if p == 0:
                                qevacs[0] = emit_qmm_sc(0, qmmdone[0])
                        if p == 0:
                            continue
                        pp = p - 1
                        par = pp % 2
                        B0, B1 = (psC, psD) if par == 0 else (psE, psF)
                        bb = 24 * par
                        qb0 = par * 512
                        em.wait("pe", SC, qevacs[pp])
                        em.wait("pe", VE, ao_done.get(pp - 2, 0))
                        for tc in range(4):
                            bank = B0 if tc % 2 == 0 else B1
                            em.inc(pe.matmul(
                                bank[:, (tc // 2) * 128:(tc // 2) * 128 + 128],
                                q_b[:, (pp % 2) * TT + tc * 128:
                                    (pp % 2) * TT + tc * 128 + 128],
                                med_bf[:, pp * 128:(pp + 1) * 128],
                                start=True, stop=True), PE)
                        sq_done = em.snap(PE)
                        em.wait("ve", PE, sq_done)
                        for tc in range(4):
                            bank = B0 if tc % 2 == 0 else B1
                            c0 = (tc // 2) * 128
                            em.inc(ve.reduce_max(bst[:, bb + 16 + 2 * tc:
                                                     bb + 17 + 2 * tc],
                                                 bank[:, c0:c0 + 64],
                                                 axis=AX.X, negate=True), VE)
                            em.inc(ve.reduce_max(bst[:, bb + 17 + 2 * tc:
                                                     bb + 18 + 2 * tc],
                                                 bank[:, c0 + 64:c0 + 128],
                                                 axis=AX.X, negate=True), VE)
                        em.wait("sc", VE)
                        em.wait("sc", PE, pet_done.get(pp - 2, 0))
                        for tc in range(4):
                            bank = B0 if tc % 2 == 0 else B1
                            c0 = (tc // 2) * 128
                            em.inc(sc.activation(
                                qm_sb[:, qb0 + tc * 128:qb0 + tc * 128 + 64],
                                bank[:, c0:c0 + 64], AF.Exp,
                                bias=bst[:, bb + 16 + 2 * tc:bb + 17 + 2 * tc],
                                scale=1.0), SC)
                            em.inc(sc.activation(
                                qm_sb[:, qb0 + tc * 128 + 64:qb0 + tc * 128 + 128],
                                bank[:, c0 + 64:c0 + 128], AF.Exp,
                                bias=bst[:, bb + 17 + 2 * tc:bb + 18 + 2 * tc],
                                scale=1.0), SC)
                        exps_done = em.snap(SC)
                        if p < NP:
                            qevacs[p] = emit_qmm_sc(p, qmmdone[p])
                        em.wait("ve", SC, exps_done)
                        for tc in range(4):
                            for hf in range(2):
                                sl = qb0 + tc * 128 + hf * 64
                                em.inc(ve.reduce_sum(
                                    bst[:, bb + 2 * tc + hf:bb + 2 * tc + hf + 1],
                                    qm_sb[:, sl:sl + 64], axis=AX.X), VE)
                        em.inc(ve.tensor_copy(jnk[:, 0:1], jnk[:, 1:2]), VE)
                        em.inc(ve.reciprocal(bst[:, bb + 8:bb + 16],
                                             bst[:, bb:bb + 8]), VE)
                        em.inc(ve.tensor_copy(jnk[:, 1:2], jnk[:, 0:1]), VE)
                        for tc in range(4):
                            for hf in range(2):
                                sl = qb0 + tc * 128 + hf * 64
                                em.inc(ve.tensor_scalar(
                                    qm_sb[:, sl:sl + 64], qm_sb[:, sl:sl + 64],
                                    bst[:, bb + 8 + 2 * tc + hf:
                                        bb + 9 + 2 * tc + hf],
                                    None, op0=ALU.mult), VE)
                        norm_done = em.snap(VE)
                        em.wait("pe", VE, norm_done)
                        for tc in range(4):
                            pt = psT1 if tc % 2 == 0 else psT2
                            c0 = par * 512 + (tc // 2) * 128
                            em.inc(pe.transpose(pt[:, c0:c0 + 128],
                                                qm_sb[:, qb0 + tc * 128:
                                                      qb0 + (tc + 1) * 128],
                                                ident_bf[:]), PE)
                        pet_done[pp] = em.snap(PE)
                        em.wait("ve", PE, pet_done[pp])
                        for tc in range(4):
                            pt = psT1 if tc % 2 == 0 else psT2
                            c0 = par * 512 + (tc // 2) * 128
                            em.inc(ve.tensor_copy(
                                qmT[:, par * 512 + tc * 128:
                                    par * 512 + (tc + 1) * 128],
                                pt[:, c0:c0 + 128]), VE)
                        qmt_done = em.snap(VE)
                        em.wait("pe", VE, qmt_done)
                        em.inc(pe.matmul(B0[:], mkv_bf[:, pp * 128:(pp + 1) * 128],
                                         qmT[:, par * 512:par * 512 + 512],
                                         start=True, stop=True), PE)
                        em.wait("ve", PE)
                        em.inc(ve.tensor_copy(ao_b[:, pp * TT:(pp + 1) * TT],
                                              B0[:]), VE)
                        ao_done[pp] = em.snap(VE)
                    for dc in range(DC):
                        em.wait("pe", VE)
                        pb = psA if dc % 2 == 0 else psB
                        for p in range(NP):
                            em.inc(pe.matmul(
                                pb[:],
                                wo_b[:, p * 1024 + dc * 128:
                                     p * 1024 + dc * 128 + 128],
                                ao_b[:, p * TT:(p + 1) * TT],
                                start=(p == 0), stop=(p == NP - 1)), PE)
                        em.wait("sc", PE)
                        em.inc(sc.activation(gscr[:, 0:512], pb[:], AF.Identity,
                                             bias=bo_c[:, dc:dc + 1],
                                             scale=1.0), SC)
                        em.wait("ve", SC)
                        xsl = x_sb[:, dc * L + t0: dc * L + t0 + TT]
                        em.inc(ve.tensor_tensor(xsl, xsl, gscr[:, 0:512],
                                                op=ALU.add), VE)

            Em(sems).run(blk, emit)

        mkv_bf = None
        med_bf = None
        if _ph in ("fa", "fab", "full"):
            with nc.sbuf_tensor("mkv_bf", [128, NP * 128], BF16) as _t1, \
                 nc.sbuf_tensor("med_bf", [128, NP * 128], BF16) as _t2:
                mkv_bf = _t1
                med_bf = _t2
                attn_a_phase()
                if _ph in ("fab", "full"):
                    attn_b_phase()
        if _ph == "full":
            ffn_phase("f2")

        with nc.Block() as blk:
            def emit(em):
                gp = em.e("gp")
                for dc in range(DC):
                    em.inc(gp.dma_start(out=outT[dc * 128:(dc + 1) * 128, :],
                                        in_=x_sb[:, dc * L:(dc + 1) * L]), SG)
                em.wait("gp", SG)
            Em(sems).run(blk, emit)

    return nc


def _host_consts(inputs):
    import ml_dtypes
    BF = ml_dtypes.bfloat16
    med = np.asarray(inputs["L0_attn_mediator"], np.float32)[0]
    med_bd = np.zeros((NP, 128, 128), np.float32)
    for p in range(NP):
        for hh in range(2):
            h = 2 * p + hh
            med_bd[p, hh * 64:(hh + 1) * 64, hh * 64:(hh + 1) * 64] = \
                med[:, h * 64:(h + 1) * 64].T
    mask = np.zeros((128, 128), np.float32)
    mask[0:64, 0:64] = 1.0
    mask[64:128, 64:128] = 1.0
    ind2 = np.zeros((2, 128), np.float32)
    ind2[0, 0:64] = 1.0
    ind2[1, 64:128] = 1.0
    d = {
        "WfcT": np.ascontiguousarray(np.asarray(inputs["L0_ffn_W_fc"], np.float32).T),
        "WpT": np.ascontiguousarray(np.asarray(inputs["L0_ffn_W_proj"], np.float32).T),
        "Wq": np.ascontiguousarray(np.asarray(inputs["L0_attn_Wq"], np.float32)),
        "Wkv": np.ascontiguousarray(np.asarray(inputs["L0_attn_Wkv"], np.float32)),
        "Wo": np.ascontiguousarray(np.asarray(inputs["L0_attn_Wo"], np.float32)),
        "bo_col": np.ascontiguousarray(
            np.asarray(inputs["L0_attn_bo"], np.float32).reshape(DC, 128).T),
        "lnw_col": np.ascontiguousarray(
            np.asarray(inputs["L0_ffn_ln_w"], np.float32).reshape(HC, 128).T),
        "lnb_col": np.ascontiguousarray(
            np.asarray(inputs["L0_ffn_ln_b"], np.float32).reshape(HC, 128).T),
        "med_bd": med_bd,
        "ident_bf": np.eye(128, dtype=np.float32).astype(BF),
        "ident_f": np.eye(128, dtype=np.float32),
        "mask_bd": mask.astype(BF),
        "ones_misc": np.ones((128, 1), np.float32),
        "ones_row": np.ones((1, 128), np.float32),
        "ind2": ind2,
    }
    return d


def kernel(**inputs):
    x = np.asarray(inputs["L0_attn_x"], np.float32)
    consts = _host_consts(inputs)
    nc = build_program()
    in_maps = []
    for c in range(8):
        m = dict(consts)
        m["xT"] = np.ascontiguousarray(x[c].T)
        in_maps.append(m)
    res = run_bass_kernel_spmd(nc, in_maps, core_ids=list(range(8)))
    out = np.stack([np.ascontiguousarray(r["outT"].T) for r in res.results], axis=0)
    return out


# version 16 (final)
# speedup vs baseline: 1.0368x; 1.0368x over previous
import os
import numpy as np
from contextlib import ExitStack

import concourse.bass as bass
import concourse.mybir as mybir
from concourse.bass_utils import run_bass_kernel_spmd

F32 = mybir.dt.float32
BF16 = mybir.dt.bfloat16
AF = mybir.ActivationFunctionType
ALU = mybir.AluOpType
AX = mybir.AxisListType

D = 1024
L = 4096
HID = 4096
DC = 8
HC = 32
NP = 8
TT = 512
NT = L // TT
AB = 256
NAB = L // AB
LNEPS = 1e-5
QEPS = 1e-6



class Sem:
    def __init__(self, h, step):
        self.h = h
        self.step = step
        self.n = 0


class _NullInstr:
    def then_inc(self, *a, **k):
        return self


_NULL = _NullInstr()


class _EngProxy:
    def __init__(self, em, name):
        self._em = em
        self._name = name

    def __getattr__(self, meth):
        em = self._em
        name = self._name

        def call(*args, **kwargs):
            if em.mode == "real" and em.cur == name:
                return getattr(em.real_eng, meth)(*args, **kwargs)
            return _NULL

        return call


class Em:

    ENGS = ("pe", "ve", "sc", "sy", "gp")

    def __init__(self, sems):
        self.sems = sems
        self.mode = "dry"
        self.cur = None
        self.real_eng = None
        self.rec = {e: [] for e in self.ENGS}
        self.pos = {e: 0 for e in self.ENGS}
        self.proxies = {e: _EngProxy(self, e) for e in self.ENGS}
        self._snaps = []
        self._snap_pos = 0

    def e(self, name):
        return self.proxies[name]

    def inc(self, instr, sem):
        if self.mode == "dry":
            sem.n += sem.step
        elif instr is not _NULL:
            instr.then_inc(sem.h, sem.step)
        return instr

    def wait(self, eng_name, sem, val=None):
        if self.mode == "dry":
            v = sem.n if val is None else val
            self.rec[eng_name].append(v)
        else:
            v = self.rec[eng_name][self.pos[eng_name]]
            self.pos[eng_name] += 1
            if self.cur == eng_name and v > 0:
                self.real_eng.wait_ge(sem.h, v)

    def snap(self, sem):
        if self.mode == "dry":
            self._snaps.append(sem.n)
            return sem.n
        v = self._snaps[self._snap_pos]
        self._snap_pos += 1
        return v

    def run(self, blk, emit_fn):
        self.mode = "dry"
        self.cur = None
        self._snap_pos = 0
        emit_fn(self)
        sections = {"pe": blk.tensor, "ve": blk.vector, "sc": blk.scalar,
                    "sy": blk.sync, "gp": blk.gpsimd}
        for name in self.ENGS:
            def body(eng, name=name):
                self.mode = "real"
                self.cur = name
                self.real_eng = eng
                self.pos = {e: 0 for e in self.ENGS}
                self._snap_pos = 0
                emit_fn(self)
            sections[name](body)
        self.mode = "dry"
        self.cur = None



def build_program():
    nc = bass.Bass("TRN2", target_bir_lowering=False,
                   detect_race_conditions=False)

    xT = nc.dram_tensor("xT", [D, L], F32, kind="ExternalInput").ap()
    WfcT = nc.dram_tensor("WfcT", [D, HID], F32, kind="ExternalInput").ap()
    WpT = nc.dram_tensor("WpT", [HID, D], F32, kind="ExternalInput").ap()
    Wq = nc.dram_tensor("Wq", [D, D], F32, kind="ExternalInput").ap()
    Wkv = nc.dram_tensor("Wkv", [D, 2 * D], F32, kind="ExternalInput").ap()
    Wo = nc.dram_tensor("Wo", [D, D], F32, kind="ExternalInput").ap()
    bo_col = nc.dram_tensor("bo_col", [128, DC], F32, kind="ExternalInput").ap()
    lnw_col = nc.dram_tensor("lnw_col", [128, HC], F32, kind="ExternalInput").ap()
    lnb_col = nc.dram_tensor("lnb_col", [128, HC], F32, kind="ExternalInput").ap()
    med_bd_d = nc.dram_tensor("med_bd", [NP, 128, 128], F32, kind="ExternalInput").ap()
    ident_bf_d = nc.dram_tensor("ident_bf", [128, 128], BF16, kind="ExternalInput").ap()
    ident_f_d = nc.dram_tensor("ident_f", [128, 128], F32, kind="ExternalInput").ap()
    mask_bd_d = nc.dram_tensor("mask_bd", [128, 128], BF16, kind="ExternalInput").ap()
    ones_d = nc.dram_tensor("ones_misc", [128, 1], F32, kind="ExternalInput").ap()
    onesrow_d = nc.dram_tensor("ones_row", [1, 128], F32, kind="ExternalInput").ap()
    ind2_d = nc.dram_tensor("ind2", [2, 128], F32, kind="ExternalInput").ap()

    outT = nc.dram_tensor("outT", [D, L], F32, kind="ExternalOutput").ap()
    _dbg = os.environ.get("KMK_DEBUG") == "1"
    if _dbg:
        dbg_xn = nc.dram_tensor("dbg_xn", [128, DC * TT], BF16, kind="ExternalOutput").ap()
        dbg_g = nc.dram_tensor("dbg_g", [128, HC * TT], BF16, kind="ExternalOutput").ap()
        dbg_v = nc.dram_tensor("dbg_v", [4, TT], F32, kind="ExternalOutput").ap()
        dbg_c = nc.dram_tensor("dbg_c", [128, 3 * DC + 2], F32, kind="ExternalOutput").ap()
        dbg_w = nc.dram_tensor("dbg_w", [128, 3 * 1024], BF16, kind="ExternalOutput").ap()
        dbg_p0 = nc.dram_tensor("dbg_p0", [128, 8], F32, kind="ExternalOutput").ap()

    sc_fc = nc.dram_tensor("sc_fc", [HC, 128, DC, 128], BF16)
    sc_p = nc.dram_tensor("sc_p", [DC, 128, HC, 128], BF16)

    with ExitStack() as top:
        et = top.enter_context

        sems = {
            "sw": Sem(et(nc.semaphore("s_sw")), 16),
            "sq": Sem(et(nc.semaphore("s_sq")), 16),
            "sg": Sem(et(nc.semaphore("s_sg")), 16),
            "wi0": Sem(et(nc.semaphore("s_wi0")), 16),
            "wi1": Sem(et(nc.semaphore("s_wi1")), 16),
            "wi2": Sem(et(nc.semaphore("s_wi2")), 16),
            "wi3": Sem(et(nc.semaphore("s_wi3")), 16),
            "wp0": Sem(et(nc.semaphore("s_wp0")), 16),
            "wp1": Sem(et(nc.semaphore("s_wp1")), 16),
            "pe": Sem(et(nc.semaphore("s_pe")), 1),
            "gpc": Sem(et(nc.semaphore("s_gpc")), 1),
            "ve": Sem(et(nc.semaphore("s_ve")), 1),
            "sc": Sem(et(nc.semaphore("s_sc")), 1),
        }
        SW, SQ, SG = sems["sw"], sems["sq"], sems["sg"]
        PE, VE, SC = sems["pe"], sems["ve"], sems["sc"]
        WI = (sems["wi0"], sems["wi1"], sems["wi2"], sems["wi3"])
        GPC = sems["gpc"]
        WP = (sems["wp0"], sems["wp1"])

        x_sb = et(nc.sbuf_tensor("x_sb", [128, DC * L], F32))
        xn_bf = et(nc.sbuf_tensor("xn_bf", [128, DC * TT], BF16))
        xbf2 = et(nc.sbuf_tensor("xbf2", [128, 2 * TT], BF16))
        x2b = et(nc.sbuf_tensor("x2b", [128, TT], BF16))
        ones_bf = et(nc.sbuf_tensor("ones_bf", [128, 1], BF16))
        ones_f = et(nc.sbuf_tensor("ones_f", [128, 1], F32))
        onesrow = et(nc.sbuf_tensor("onesrow", [1, 128], F32))
        ind2 = et(nc.sbuf_tensor("ind2_sb", [2, 128], F32))
        ident_bf = et(nc.sbuf_tensor("ident_bf_sb", [128, 128], BF16))
        ident_f = et(nc.sbuf_tensor("ident_f_sb", [128, 128], F32))
        mask_bd = et(nc.sbuf_tensor("mask_bd_sb", [128, 128], BF16))
        lnw_c = et(nc.sbuf_tensor("lnw_c", [128, HC], F32))
        lnb_c = et(nc.sbuf_tensor("lnb_c", [128, HC], F32))
        bo_c = et(nc.sbuf_tensor("bo_c", [128, DC], F32))
        c1_c = et(nc.sbuf_tensor("c1_c", [128, DC], F32))
        c2h_c = et(nc.sbuf_tensor("c2h_c", [128, DC], F32))
        vec_a = et(nc.sbuf_tensor("vec_a", [1, TT], F32))
        vec_b = et(nc.sbuf_tensor("vec_b", [1, TT], F32))
        vec_c = et(nc.sbuf_tensor("vec_c", [1, TT], F32))
        thr_b = et(nc.sbuf_tensor("thr_b", [128, 2], F32))
        one1 = et(nc.sbuf_tensor("one1", [1, 1], F32))
        jnk = et(nc.sbuf_tensor("jnk", [128, 2], F32))
        eps1 = et(nc.sbuf_tensor("eps1", [1, 1], F32))

        psA = et(nc.psum_tensor("psA", [128, 512], F32))
        psB = et(nc.psum_tensor("psB", [128, 512], F32))
        psC = et(nc.psum_tensor("psC", [128, 512], F32))
        psD = et(nc.psum_tensor("psD", [128, 512], F32))
        psE = et(nc.psum_tensor("psE", [128, 512], F32))
        psF = et(nc.psum_tensor("psF", [128, 512], F32))
        psT1 = et(nc.psum_tensor("psT1", [128, 1024], BF16))
        psT2 = et(nc.psum_tensor("psT2", [128, 1024], BF16))

        def emit_stream_ln(em, t0, tw, scr, gp_apply=False, msb=None, rsb=None,
                           expln=False, castbuf=None):
            pe, ve, sc = em.e("pe"), em.e("ve"), em.e("sc")
            gp = em.e("gp")
            em.wait("ve", PE)
            for dc in range(DC):
                em.inc(ve.tensor_copy(castbuf[:, dc * tw:(dc + 1) * tw],
                                      x_sb[:, dc * L + t0: dc * L + t0 + tw]), VE)
            casts_done = em.snap(VE)
            em.wait("pe", VE, casts_done)
            for dc in range(DC):
                em.inc(pe.matmul(psE[0:1, 0:tw], ones_bf[:],
                                 castbuf[:, dc * tw:(dc + 1) * tw],
                                 start=(dc == 0), stop=(dc == DC - 1)), PE)
            sums_done = em.snap(PE)
            em.wait("ve", PE, sums_done)
            for dc in range(DC):
                cb = castbuf[:, dc * tw:(dc + 1) * tw]
                em.inc(ve.tensor_tensor(cb, cb, cb, op=ALU.mult), VE)
            sq_done = em.snap(VE)
            em.wait("pe", VE, sq_done)
            for dc in range(DC):
                em.inc(pe.matmul(psF[0:1, 0:tw], ones_bf[:],
                                 castbuf[:, dc * tw:(dc + 1) * tw],
                                 start=(dc == 0), stop=(dc == DC - 1)), PE)
            em.wait("ve", PE)
            em.inc(ve.tensor_scalar(vec_a[:, 0:tw], psE[0:1, 0:tw], 1.0 / D, None,
                                    op0=ALU.mult), VE)
            em.inc(ve.tensor_scalar(vec_b[:, 0:tw], psF[0:1, 0:tw], 1.0 / D, None,
                                    op0=ALU.mult), VE)
            em.inc(ve.tensor_tensor(vec_c[:, 0:tw], vec_a[:, 0:tw], vec_a[:, 0:tw],
                                    op=ALU.mult), VE)
            em.inc(ve.tensor_tensor(vec_b[:, 0:tw], vec_b[:, 0:tw], vec_c[:, 0:tw],
                                    op=ALU.subtract), VE)
            em.wait("sc", VE)
            if expln:
                em.inc(sc.activation(vec_b[:, 0:tw], vec_b[:, 0:tw], AF.Ln,
                                     bias=eps1[:], scale=1.0), SC)
                em.inc(sc.activation(vec_c[:, 0:tw], vec_b[:, 0:tw], AF.Exp,
                                     bias=0.0, scale=-0.5), SC)
                em.wait("pe", SC)
                em.wait("ve", SC)
            else:
                em.inc(sc.activation(vec_b[:, 0:tw], vec_b[:, 0:tw], AF.Sqrt,
                                     bias=eps1[:], scale=1.0), SC)
                em.wait("ve", SC)
                em.inc(ve.reciprocal(vec_c[:, 0:tw], vec_b[:, 0:tw]), VE)
                em.wait("pe", VE)
            em.inc(pe.matmul(psC[:, 0:tw], onesrow[:], vec_a[:, 0:tw],
                             start=True, stop=True), PE)
            em.inc(pe.matmul(psD[:, 0:tw], onesrow[:], vec_c[:, 0:tw],
                             start=True, stop=True), PE)
            if gp_apply:
                em.wait("sc", PE)
                em.inc(sc.activation(msb[:, 0:tw], psC[:, 0:tw], AF.Copy), SC)
                em.inc(sc.activation(rsb[:, 0:tw], psD[:, 0:tw], AF.Copy), SC)
                em.wait("ve", SC)
                for dc in range(DC):
                    xsl = x_sb[:, dc * L + t0: dc * L + t0 + tw]
                    em.inc(ve.tensor_tensor(msb[:, tw:2 * tw], xsl,
                                            msb[:, 0:tw],
                                            op=ALU.subtract), VE)
                    em.inc(ve.tensor_tensor(xn_bf[:, dc * TT: dc * TT + tw],
                                            msb[:, tw:2 * tw], rsb[:, 0:tw],
                                            op=ALU.mult), VE)
                em.inc(ve.tensor_copy(jnk[:, 0:1], jnk[:, 1:2]), GPC)
            else:
                em.wait("ve", PE)
                for dc in range(DC):
                    xsl = x_sb[:, dc * L + t0: dc * L + t0 + tw]
                    em.inc(ve.tensor_tensor(scr[:, 0:tw], xsl, psC[:, 0:tw],
                                            op=ALU.subtract), VE)
                    em.inc(ve.tensor_tensor(xn_bf[:, dc * TT: dc * TT + tw],
                                            scr[:, 0:tw], psD[:, 0:tw],
                                            op=ALU.mult), VE)

        def emit_ffn_ln(em, ti, scr):
            pe, ve, sc = em.e("pe"), em.e("ve"), em.e("sc")
            t0 = ti * TT
            tw = TT
            pT1 = psT1[:].bitcast(F32)
            pT2 = psT2[:].bitcast(F32)
            em.wait("ve", PE)
            for dc in range(DC):
                em.inc(ve.tensor_copy(xn_bf[:, dc * TT:dc * TT + tw],
                                      x_sb[:, dc * L + t0: dc * L + t0 + tw]), VE)
            casts_done = em.snap(VE)
            em.wait("pe", VE, casts_done)
            for dc in range(DC):
                em.inc(pe.matmul(psE[0:1, 0:tw], ones_bf[:],
                                 xn_bf[:, dc * TT:dc * TT + tw],
                                 start=(dc == 0), stop=(dc == DC - 1)), PE)
            sums_done = em.snap(PE)
            em.wait("sc", PE, sums_done)
            for dc in range(DC):
                cb = xn_bf[:, dc * TT:dc * TT + tw]
                em.inc(sc.activation(cb, cb, AF.Square), SC)
            sq_done = em.snap(SC)
            em.wait("pe", SC, sq_done)
            for dc in range(DC):
                em.inc(pe.matmul(psF[0:1, 0:tw], ones_bf[:],
                                 xn_bf[:, dc * TT:dc * TT + tw],
                                 start=(dc == 0), stop=(dc == DC - 1)), PE)
            lnstats_done = em.snap(PE)
            em.wait("ve", PE, lnstats_done)
            em.inc(ve.tensor_scalar(vec_a[:], psE[0:1, :], 1.0 / D, None,
                                    op0=ALU.mult), VE)
            em.inc(ve.tensor_scalar(vec_b[:], psF[0:1, :], 1.0 / D, None,
                                    op0=ALU.mult), VE)
            em.inc(ve.tensor_tensor(vec_c[:], vec_a[:], vec_a[:],
                                    op=ALU.mult), VE)
            em.inc(ve.tensor_tensor(vec_b[:], vec_b[:], vec_c[:],
                                    op=ALU.subtract), VE)
            em.wait("sc", VE)
            em.inc(sc.activation(vec_b[:], vec_b[:], AF.Sqrt,
                                 bias=eps1[:], scale=1.0), SC)
            em.wait("ve", SC)
            em.inc(ve.reciprocal(vec_c[:], vec_b[:]), VE)
            em.wait("pe", VE)
            em.inc(pe.matmul(pT1[:, 0:tw], onesrow[:], vec_a[:],
                             start=True, stop=True), PE)
            em.inc(pe.matmul(pT2[:, 0:tw], onesrow[:], vec_c[:],
                             start=True, stop=True), PE)
            bca = em.snap(PE)
            em.wait("ve", PE, bca)
            for dc in range(DC):
                xsl = x_sb[:, dc * L + t0: dc * L + t0 + tw]
                em.inc(ve.tensor_tensor(scr[:, 0:tw], xsl, pT1[:, 0:tw],
                                        op=ALU.subtract), VE)
                em.inc(ve.tensor_tensor(xn_bf[:, dc * TT: dc * TT + tw],
                                        scr[:, 0:tw], pT2[:, 0:tw],
                                        op=ALU.mult), VE)

        with nc.sbuf_tensor("p0_raw", [128, 4096], F32) as raw, \
             nc.sbuf_tensor("p0_raw2", [128, 4096], F32) as raw2, \
             nc.sbuf_tensor("p0_t1", [128, 2048], F32) as t1, \
             nc.sbuf_tensor("p0_t2", [128, 2048], F32) as t2, \
             nc.sbuf_tensor("p0_q", [128, 2048], BF16) as qb, \
             nc.sbuf_tensor("p0_acc", [128, 34], F32) as acc, \
             nc.sbuf_tensor("p0_cacc", [128, 16], F32) as cacc, \
             nc.Block() as blk:

            def emit(em):
                pe, ve, sc, sy, gp = (em.e("pe"), em.e("ve"), em.e("sc"),
                                      em.e("sy"), em.e("gp"))
                for dc in range(DC):
                    em.inc(gp.dma_start(out=x_sb[:, dc * L:(dc + 1) * L],
                                        in_=xT[dc * 128:(dc + 1) * 128, :]), SG)
                em.inc(sy.dma_start(out=ident_bf[:], in_=ident_bf_d), SW)
                em.inc(sy.dma_start(out=ident_f[:], in_=ident_f_d), SW)
                em.inc(sy.dma_start(out=mask_bd[:], in_=mask_bd_d), SW)
                em.inc(sy.dma_start(out=onesrow[:], in_=onesrow_d), SW)
                em.inc(sy.dma_start(out=ind2[:], in_=ind2_d), SW)
                em.inc(sy.dma_start(out=lnw_c[:], in_=lnw_col), SW)
                em.inc(sy.dma_start(out=lnb_c[:], in_=lnb_col), SW)
                em.inc(sy.dma_start(out=bo_c[:], in_=bo_col), SW)
                em.inc(sy.dma_start(out=ones_f[:], in_=ones_d), SW)
                med_done = em.snap(SW)
                em.wait("ve", SW, med_done)
                em.inc(ve.tensor_copy(ones_bf[:], ones_f[:]), VE)
                em.inc(ve.memset(cacc[:], 0.0), VE)
                em.inc(ve.memset(eps1[:], LNEPS), VE)
                em.inc(ve.memset(jnk[:], 0.0), VE)

                em.inc(ve.memset(acc[:, 0:1], 0.0), VE)
                em.inc(ve.memset(acc[:, 1:2], 0.0), VE)
                absF = {}
                for dc in range(DC):
                    rb = raw if dc % 2 == 0 else raw2
                    if dc >= 2:
                        em.wait("sy", SC, absF[dc - 2])
                    em.inc(sy.dma_start(out=rb[:],
                                        in_=WfcT[dc * 128:(dc + 1) * 128, :]), SW)
                    em.wait("sc", SW)
                    if dc >= 1:
                        em.wait("sc", VE)
                    em.inc(sc.activation(t1[:], rb[:, 0:2048], AF.Abs,
                                         accum_out=acc[:, 2:3]), SC)
                    em.inc(sc.activation(t2[:], rb[:, 2048:4096], AF.Abs,
                                         accum_out=acc[:, 3:4]), SC)
                    absF[dc] = SC.n if em.mode == "dry" else 0
                    em.wait("ve", SC)
                    em.inc(ve.tensor_tensor(acc[:, 0:1], acc[:, 0:1], acc[:, 2:3],
                                            op=ALU.add), VE)
                    em.inc(ve.tensor_tensor(acc[:, 1:2], acc[:, 1:2], acc[:, 3:4],
                                            op=ALU.add), VE)
                em.inc(ve.tensor_copy(jnk[:, 0:1], jnk[:, 1:2]), VE)
                em.inc(ve.tensor_tensor(acc[:, 4:5], acc[:, 0:1], acc[:, 1:2],
                                        op=ALU.add), VE)
                em.wait("pe", VE)
                em.inc(pe.matmul(psB[0:1, 0:1], ones_f[:], acc[:, 4:5],
                                 start=True, stop=True), PE)
                em.wait("ve", PE)
                em.inc(ve.tensor_scalar(one1[:], psB[0:1, 0:1], 0.5 / (D * HID),
                                        0.5 * QEPS, op0=ALU.mult, op1=ALU.add), VE)
                em.wait("pe", VE)
                em.inc(pe.matmul(psB[:, 0:1], onesrow[:], one1[:],
                                 start=True, stop=True), PE)
                em.wait("ve", PE)
                em.inc(ve.tensor_copy(thr_b[:, 0:1], psB[:, 0:1]), VE)
                em.inc(ve.tensor_scalar(thr_b[:, 1:2], psB[:, 0:1], -1.0, None,
                                        op0=ALU.mult), VE)
                thrF = em.snap(VE)

                vqF = {}
                for dc in range(DC):
                    for h in range(2):
                        k = dc * 2 + h
                        rb = raw if k % 2 == 0 else raw2
                        if k == 0:
                            em.wait("sy", VE, thrF)
                        elif k >= 2:
                            em.wait("sy", VE, vqF[k - 2])
                        em.inc(sy.dma_start(
                            out=rb[:, 0:2048],
                            in_=WfcT[dc * 128:(dc + 1) * 128,
                                     h * 2048:(h + 1) * 2048]), SW)
                        em.wait("ve", SW)
                        em.inc(ve.tensor_scalar(t1[:], rb[:, 0:2048],
                                                thr_b[:, 0:1], None,
                                                op0=ALU.is_gt), VE)
                        em.inc(ve.tensor_scalar(t2[:], rb[:, 0:2048],
                                                thr_b[:, 1:2], None,
                                                op0=ALU.is_lt), VE)
                        vqF[k] = VE.n if em.mode == "dry" else 0
                        em.wait("ve", SG)
                        em.inc(ve.tensor_tensor(qb[:], t1[:], t2[:],
                                                op=ALU.subtract), VE)
                        em.wait("gp", VE)
                        dst = sc_fc.ap()[h * 16:(h + 1) * 16, :, dc, :]
                        dst = dst.transpose([1, 0, 2])
                        em.inc(gp.dma_start(
                            out=dst,
                            in_=qb[:].rearrange("p (a b) -> p a b", b=128)), SG)

                em.inc(ve.memset(acc[:, 0:1], 0.0), VE)
                em.inc(ve.memset(acc[:, 1:2], 0.0), VE)
                absP = {}
                for hc in range(HC):
                    rb = raw if hc % 2 == 0 else raw2
                    if hc < 2:
                        em.wait("sy", VE)
                    else:
                        em.wait("sy", SC, absP[hc - 2])
                    em.inc(sy.dma_start(out=rb[:, 0:1024],
                                        in_=WpT[hc * 128:(hc + 1) * 128, :]), SW)
                    em.wait("sc", SW)
                    em.wait("sc", VE)
                    em.inc(sc.activation(t1[:, 0:1024], rb[:, 0:1024], AF.Abs,
                                         accum_out=acc[:, 2 + hc % 2:3 + hc % 2]),
                           SC)
                    absP[hc] = SC.n if em.mode == "dry" else 0
                    em.wait("ve", SC)
                    em.inc(ve.tensor_tensor(acc[:, hc % 2:1 + hc % 2],
                                            acc[:, hc % 2:1 + hc % 2],
                                            acc[:, 2 + hc % 2:3 + hc % 2],
                                            op=ALU.add), VE)
                em.inc(ve.tensor_copy(jnk[:, 0:1], jnk[:, 1:2]), VE)
                em.inc(ve.tensor_tensor(acc[:, 4:5], acc[:, 0:1], acc[:, 1:2],
                                        op=ALU.add), VE)
                em.wait("pe", VE)
                em.inc(pe.matmul(psB[0:1, 0:1], ones_f[:], acc[:, 4:5],
                                 start=True, stop=True), PE)
                em.wait("ve", PE)
                em.inc(ve.tensor_scalar(one1[:], psB[0:1, 0:1], 0.5 / (D * HID),
                                        0.5 * QEPS, op0=ALU.mult, op1=ALU.add), VE)
                em.wait("pe", VE)
                em.inc(pe.matmul(psB[:, 0:1], onesrow[:], one1[:],
                                 start=True, stop=True), PE)
                em.wait("ve", PE)
                em.inc(ve.tensor_copy(thr_b[:, 0:1], psB[:, 0:1]), VE)
                em.inc(ve.tensor_scalar(thr_b[:, 1:2], psB[:, 0:1], -1.0, None,
                                        op0=ALU.mult), VE)
                thrP = em.snap(VE)

                vqP = {}
                for hc in range(HC):
                    rb = raw if hc % 2 == 0 else raw2
                    if hc == 0:
                        em.wait("sy", VE, thrP)
                    elif hc >= 2:
                        em.wait("sy", VE, vqP[hc - 2])
                    em.inc(sy.dma_start(out=rb[:, 0:1024],
                                        in_=WpT[hc * 128:(hc + 1) * 128, :]), SW)
                    em.wait("ve", SW)
                    em.wait("ve", PE)
                    em.inc(ve.tensor_scalar(t1[:, 0:1024], rb[:, 0:1024],
                                            thr_b[:, 0:1], None, op0=ALU.is_gt),
                           VE)
                    em.inc(ve.tensor_scalar(t2[:, 0:1024], rb[:, 0:1024],
                                            thr_b[:, 1:2], None, op0=ALU.is_lt),
                           VE)
                    vqP[hc] = VE.n if em.mode == "dry" else 0
                    em.inc(ve.tensor_tensor(t1[:, 0:1024], t1[:, 0:1024],
                                            t2[:, 0:1024], op=ALU.subtract), VE)
                    em.wait("ve", SG)
                    em.inc(ve.tensor_scalar(qb[:, 0:1024], t1[:, 0:1024],
                                            lnw_c[:, hc:hc + 1], None,
                                            op0=ALU.mult), VE)
                    ve_done = em.snap(VE)
                    em.wait("pe", VE, ve_done)
                    for dcq in range(DC):
                        em.inc(pe.matmul(psE[:, dcq:dcq + 1],
                                         t1[:, dcq * 128:(dcq + 1) * 128],
                                         lnb_c[:, hc:hc + 1],
                                         start=True, stop=True,
                                         skip_group_check=True), PE)
                    for dcq in range(DC):
                        em.inc(pe.matmul(psF[:, dcq:dcq + 1],
                                         qb[:, dcq * 128:(dcq + 1) * 128],
                                         ones_bf[:],
                                         start=True, stop=True,
                                         skip_group_check=True), PE)
                    em.wait("ve", PE)
                    em.inc(ve.tensor_tensor(cacc[:, 0:8], cacc[:, 0:8],
                                            psE[:, 0:8], op=ALU.add), VE)
                    em.inc(ve.tensor_tensor(cacc[:, 8:16], cacc[:, 8:16],
                                            psF[:, 0:8], op=ALU.add), VE)
                    em.wait("gp", VE)
                    dst = sc_p.ap()[:, :, hc, :].transpose([1, 0, 2])
                    em.inc(gp.dma_start(
                        out=dst,
                        in_=qb[:, 0:1024].rearrange("p (a b) -> p a b", b=128)), SG)
                em.inc(ve.tensor_scalar(c2h_c[:], cacc[:, 0:8], 0.5, None,
                                        op0=ALU.mult), VE)
                em.inc(ve.tensor_copy(c1_c[:], cacc[:, 8:16]), VE)
                if _dbg:
                    em.wait("gp", VE)
                    em.inc(em.e("gp").dma_start(out=dbg_p0, in_=acc[:, 0:8]), SG)
                em.wait("gp", SG)
                em.wait("sy", SW)

            Em(sems).run(blk, emit)

        def ffn_phase(tag):
            with nc.sbuf_tensor(f"{tag}_g", [128, HC * TT], BF16) as g_sb, \
                 nc.sbuf_tensor(f"{tag}_wi", [128, 3 * 1024], BF16) as wfc_b, \
                 nc.sbuf_tensor(f"{tag}_wp", [128, 2 * 4096], BF16) as wp_b, \
                 nc.sbuf_tensor(f"{tag}_scr", [128, 512], F32) as scr, \
                 nc.sbuf_tensor(f"{tag}_ub", [128, 512], F32) as ub_sb, \
                 nc.Block() as blk:

                def emit(em):
                    pe, ve, sc, sy = em.e("pe"), em.e("ve"), em.e("sc"), em.e("sy")
                    in_dma = {}
                    pj_dma = {}
                    pe_grp = {}
                    pe_pj = {}
                    ve_epi = {}
                    nwi = [0]
                    npj = [0]

                    def issue_wi():
                        k = nwi[0]
                        if k >= NT * HC:
                            return
                        nwi[0] += 1
                        if k >= 3:
                            em.wait("sy", PE, pe_grp.get(k - 3, 0))
                        wsem = WI[k % 3]
                        hc = k % HC
                        em.inc(sy.dma_start(
                            out=wfc_b[:, (k % 3) * 1024:(k % 3) * 1024 + 1024],
                            in_=sc_fc.ap()[hc].rearrange("p a b -> p (a b)")),
                            wsem)
                        in_dma[k] = wsem.n if em.mode == "dry" else 0

                    def issue_pj():
                        k = npj[0]
                        if k >= NT * DC:
                            return
                        npj[0] += 1
                        if k >= 2:
                            em.wait("sc", PE, pe_pj.get(k - 2, 0))
                        wsem = WP[k % 2]
                        dc = k % DC
                        em.inc(sc.dma_start(
                            out=wp_b[:, (k % 2) * 4096:(k % 2) * 4096 + 4096],
                            in_=sc_p.ap()[dc].rearrange("p a b -> p (a b)")),
                            wsem)
                        pj_dma[k] = wsem.n if em.mode == "dry" else 0

                    for _ in range(3):
                        issue_wi()
                    for _ in range(2):
                        issue_pj()

                    emit_ffn_ln(em, 0, scr)
                    for ti in range(NT):
                        t0 = ti * TT
                        sc_evac = {}
                        pe_stats = {}
                        for hc in range(HC + 1):
                            if hc < HC:
                                k = ti * HC + hc
                                em.wait("pe", WI[k % 3], in_dma.get(k, 0))
                                if hc == 0:
                                    em.wait("pe", VE)
                                if hc >= 2:
                                    em.wait("pe", SC, sc_evac.get(hc - 2, 0))
                                pb = psA if hc % 2 == 0 else psB
                                wsl = wfc_b[:, (k % 3) * 1024:(k % 3) * 1024 + 1024]
                                for dc in range(DC):
                                    em.inc(pe.matmul(
                                        pb[:], wsl[:, dc * 128:(dc + 1) * 128],
                                        xn_bf[:, dc * TT:(dc + 1) * TT],
                                        start=(dc == 0), stop=(dc == DC - 1)), PE)
                                pe_grp[k] = PE.n if em.mode == "dry" else 0
                                issue_wi()
                            if hc >= 1:
                                hp = hc - 1
                                em.wait("pe", SC, sc_evac.get(hp, 0))
                                em.inc(pe.matmul(psE[0:1, :], ones_bf[:],
                                                 g_sb[:, hp * TT:(hp + 1) * TT],
                                                 start=(hp == 0),
                                                 stop=(hp == HC - 1)), PE)
                                em.inc(pe.matmul(psF[0:1, :], ones_bf[:], x2b[:],
                                                 start=(hp == 0),
                                                 stop=(hp == HC - 1)), PE)
                                pe_stats[hp] = PE.n if em.mode == "dry" else 0
                            if hc < HC:
                                em.wait("sc", PE, pe_grp.get(k, 0))
                                em.inc(sc.activation(
                                    g_sb[:, hc * TT:(hc + 1) * TT],
                                    pb[:], AF.Gelu), SC)
                                if hc >= 1:
                                    em.wait("sc", PE, pe_stats.get(hc - 1, 0))
                                em.inc(sc.activation(
                                    x2b[:],
                                    g_sb[:, hc * TT:(hc + 1) * TT],
                                    AF.Square), SC)
                                sc_evac[hc] = SC.n if em.mode == "dry" else 0
                        stats_done = em.snap(PE)
                        em.wait("ve", PE, stats_done)
                        em.inc(ve.tensor_scalar(vec_a[:], psE[0:1, :],
                                                1.0 / HID, None,
                                                op0=ALU.mult), VE)
                        em.inc(ve.tensor_scalar(vec_b[:], psF[0:1, :],
                                                1.0 / HID, None,
                                                op0=ALU.mult), VE)
                        em.inc(ve.tensor_tensor(vec_c[:], vec_a[:],
                                                vec_a[:], op=ALU.mult), VE)
                        em.inc(ve.tensor_tensor(vec_b[:], vec_b[:],
                                                vec_c[:],
                                                op=ALU.subtract), VE)
                        em.wait("sc", VE)
                        em.inc(sc.activation(vec_b[:], vec_b[:], AF.Sqrt,
                                             bias=eps1[:], scale=1.0), SC)
                        em.wait("ve", SC)
                        em.inc(ve.reciprocal(vec_c[:], vec_b[:]), VE)
                        em.inc(ve.tensor_scalar(vec_c[:], vec_c[:], 0.5,
                                                None, op0=ALU.mult), VE)
                        em.inc(ve.tensor_tensor(vec_b[:], vec_c[:],
                                                vec_a[:], op=ALU.mult), VE)
                        em.inc(ve.tensor_scalar(vec_b[:], vec_b[:], -1.0,
                                                None, op0=ALU.mult), VE)
                        hfin_done = em.snap(VE)
                        em.wait("pe", VE, hfin_done)
                        em.inc(pe.matmul(psC[:], onesrow[:], vec_c[:],
                                         start=True, stop=True), PE)
                        em.inc(pe.matmul(psD[:], onesrow[:], vec_b[:],
                                         start=True, stop=True), PE)
                        em.wait("ve", PE)
                        em.inc(ve.tensor_copy(ub_sb[:], psC[:]), VE)
                        if ti + 1 < NT:
                            emit_ffn_ln(em, ti + 1, scr)
                        for dc in range(DC):
                            k = ti * DC + dc
                            em.wait("pe", WP[k % 2], pj_dma.get(k, 0))
                            if dc >= 2:
                                em.wait("pe", VE, ve_epi.get(dc - 2, 0))
                            pb = psA if dc % 2 == 0 else psB
                            wsl = wp_b[:, (k % 2) * 4096:(k % 2) * 4096 + 4096]
                            for hc in range(HC):
                                em.inc(pe.matmul(
                                    pb[:], wsl[:, hc * 128:(hc + 1) * 128],
                                    g_sb[:, hc * TT:(hc + 1) * TT],
                                    start=(hc == 0), stop=(hc == HC - 1)), PE)
                            pe_pj[k] = PE.n if em.mode == "dry" else 0
                            issue_pj()
                            xsl = x_sb[:, dc * L + t0: dc * L + t0 + TT]
                            em.wait("ve", PE)
                            em.inc(ve.tensor_tensor(scr[:], pb[:], ub_sb[:],
                                                    op=ALU.mult), VE)
                            em.inc(ve.tensor_tensor(xsl, xsl, scr[:],
                                                    op=ALU.add), VE)
                            em.wait("sc", VE)
                            em.inc(sc.activation(scr[:], psD[:],
                                                 AF.Identity,
                                                 bias=c2h_c[:, dc:dc + 1],
                                                 scale=c1_c[:, dc:dc + 1]), SC)
                            em.wait("ve", SC)
                            em.inc(ve.tensor_tensor(xsl, xsl, scr[:],
                                                    op=ALU.add), VE)
                            ve_epi[dc] = VE.n if em.mode == "dry" else 0

                    if _dbg and tag == "f1":
                        gp = em.e("gp")
                        em.wait("gp", VE)
                        em.wait("gp", SC)
                        em.inc(gp.dma_start(out=dbg_xn, in_=xn_bf[:]), SG)
                        em.inc(gp.dma_start(out=dbg_g, in_=g_sb[:]), SG)
                        em.inc(gp.dma_start(out=dbg_v[0:1, :], in_=vec_a[:]), SG)
                        em.inc(gp.dma_start(out=dbg_v[1:2, :], in_=vec_b[:]), SG)
                        em.inc(gp.dma_start(out=dbg_v[2:3, :], in_=vec_c[:]), SG)
                        em.inc(gp.dma_start(out=dbg_v[3:4, :], in_=ub_sb[0:1, :]), SG)
                        em.inc(gp.dma_start(out=dbg_c[:, 0:DC], in_=c1_c[:]), SG)
                        em.inc(gp.dma_start(out=dbg_c[:, DC:2 * DC], in_=c2h_c[:]), SG)
                        em.inc(gp.dma_start(out=dbg_c[:, 2 * DC:3 * DC], in_=bo_c[:]), SG)
                        em.inc(gp.dma_start(out=dbg_c[:, 3 * DC:3 * DC + 2], in_=thr_b[:]), SG)
                        em.inc(gp.dma_start(out=dbg_w[:, 0:1024], in_=wfc_b[:, 0:1024]), SG)
                        em.inc(gp.dma_start(out=xn_bf[:, 0:1024],
                                            in_=sc_fc.ap()[0].rearrange("p a b -> p (a b)")), SG)
                        em.wait("gp", SG)
                        em.inc(gp.dma_start(out=dbg_w[:, 1024:2048], in_=xn_bf[:, 0:1024]), SG)
                        em.wait("gp", SG)

                Em(sems).run(blk, emit)

        _ph = os.environ.get("KMK_PHASES", "full")
        if _ph != "0":
            ffn_phase("f1")

        def attn_a_phase():
          with nc.sbuf_tensor("a_wkv", [128, DC * 2048], BF16) as wkv_b, \
             nc.sbuf_tensor("a_k", [128, NP * AB], BF16) as k_b, \
             nc.sbuf_tensor("a_v", [128, (AB // 128) * 1024], BF16) as v_b, \
             nc.sbuf_tensor("a_es", [128, NP * AB], BF16) as es_b, \
             nc.sbuf_tensor("a_est", [128, AB], BF16) as est_b, \
             nc.sbuf_tensor("a_mkv", [128, NP * 128], F32) as mkv_sb, \
             nc.sbuf_tensor("a_stat", [128, 56], F32) as ast, \
             nc.sbuf_tensor("a_scr", [128, 1024], F32) as scr, \
             nc.Block() as blk:

            def emit(em):
                pe, ve, sc, sy = em.e("pe"), em.e("ve"), em.e("sc"), em.e("sy")
                for p in range(NP):
                    if p > 0:
                        em.wait("sy", SC)
                    em.inc(sy.dma_start(out=scr[:, 0:128].bitcast(F32)
                                        if False else scr[:, 0:128],
                                        in_=med_bd_d[p]), SW)
                    em.wait("sc", SW)
                    em.inc(sc.activation(med_bf[:, p * 128:(p + 1) * 128],
                                         scr[:, 0:128],
                                         AF.Copy, scale=0.125), SC)
                wkv_cp = {}
                for k in range(DC * 2):
                    dic, h = k // 2, k % 2
                    slot = k % 2
                    bounce = scr[:, 0:1024] if slot == 0 else mkv_sb[:]
                    qeng = "sy" if slot == 0 else "sc"
                    eng = sy if slot == 0 else sc
                    qsem = SW if slot == 0 else WI[1]
                    if k >= 2:
                        em.wait(qeng, VE, wkv_cp[k - 2])
                    if k == 0:
                        em.wait("sy", SC)
                    em.inc(eng.dma_start(
                        out=bounce,
                        in_=Wkv[dic * 128:(dic + 1) * 128,
                                h * 1024:(h + 1) * 1024]), qsem)
                    em.wait("ve", qsem)
                    em.inc(ve.tensor_copy(
                        wkv_b[:, dic * 2048 + h * 1024:
                              dic * 2048 + h * 1024 + 1024],
                        bounce), VE)
                    wkv_cp[k] = VE.n if em.mode == "dry" else 0
                em.inc(ve.memset(ast[:, 0:8], -1.0e30), VE)
                em.inc(ve.memset(ast[:, 8:16], 0.0), VE)
                em.inc(ve.memset(mkv_sb[:], 0.0), VE)
                wkv_done = em.snap(VE)
                em.wait("pe", VE, wkv_done)

                sc_exp = {}

                KVB = (psA, psB, psC, psD)

                def emit_kv(b):
                    for p in range(NP):
                        pb = KVB[p % 4]
                        if p == 2 or p >= 4:
                            em.wait("pe", SC)
                        for dic in range(DC):
                            em.inc(pe.matmul(
                                pb[:, 0:AB],
                                wkv_b[:, dic * 2048 + p * 128:
                                      dic * 2048 + p * 128 + 128],
                                xn_bf[:, dic * TT:dic * TT + AB],
                                start=(dic == 0), stop=(dic == DC - 1)), PE)
                        em.wait("sc", PE)
                        em.inc(sc.activation(k_b[:, p * AB:(p + 1) * AB],
                                             pb[:, 0:AB], AF.Copy), SC)
                    for lc in range(AB // 128):
                        for hf in range(2):
                            i = lc * 2 + hf
                            pb = KVB[i % 4]
                            em.wait("pe", SC)
                            for dic in range(DC):
                                em.inc(pe.matmul(
                                    pb[:],
                                    xn_bf[:, dic * TT + lc * 128:
                                          dic * TT + lc * 128 + 128],
                                    wkv_b[:, dic * 2048 + 1024 + hf * 512:
                                          dic * 2048 + 1024 + hf * 512 + 512],
                                    start=(dic == 0), stop=(dic == DC - 1)), PE)
                            em.wait("sc", PE)
                            em.inc(sc.activation(
                                v_b[:, lc * 1024 + hf * 512:
                                    lc * 1024 + hf * 512 + 512],
                                pb[:], AF.Copy), SC)

                def emit_sk_softmax(b):
                    SKB = (psA, psB, psC, psD)
                    em.wait("pe", SC)
                    em.wait("pe", VE)
                    for p in range(NP):
                        pb = SKB[p % 4]
                        if p >= 4:
                            em.wait("pe", SC, sc_exp.get((b, p - 4), 0))
                        em.inc(pe.matmul(pb[:, 0:AB],
                                         med_bf[:, p * 128:(p + 1) * 128],
                                         k_b[:, p * AB:(p + 1) * AB],
                                         start=True, stop=True), PE)
                        sk_done = em.snap(PE)
                        em.wait("ve", PE, sk_done)
                        em.inc(ve.reduce_max(ast[:, 48 + p:49 + p], pb[:, 0:AB],
                                             axis=AX.X), VE)
                        em.inc(ve.tensor_copy(ast[:, 16 + p:17 + p],
                                              ast[:, p:p + 1]), VE)
                        em.inc(ve.tensor_copy(jnk[:, 0:1], jnk[:, 1:2]), VE)
                        em.inc(ve.tensor_scalar(ast[:, 24 + p:25 + p],
                                                ast[:, 48 + p:49 + p],
                                                ast[:, 16 + p:17 + p], -1.0,
                                                op0=ALU.max, op1=ALU.mult), VE)
                        em.inc(ve.tensor_scalar(ast[:, p:p + 1],
                                                ast[:, 48 + p:49 + p],
                                                ast[:, 16 + p:17 + p], None,
                                                op0=ALU.max), VE)
                        em.wait("sc", VE)
                        em.inc(sc.activation(ast[:, 32 + p:33 + p],
                                             ast[:, 16 + p:17 + p], AF.Exp,
                                             bias=ast[:, 24 + p:25 + p],
                                             scale=1.0), SC)
                        em.inc(sc.activation(es_b[:, p * AB:(p + 1) * AB],
                                             pb[:, 0:AB], AF.Exp,
                                             bias=ast[:, 24 + p:25 + p],
                                             scale=1.0,
                                             accum_out=ast[:, 40 + p:41 + p]), SC)
                        sc_exp[(b, p)] = SC.n if em.mode == "dry" else 0
                    em.wait("ve", SC)
                    for p in range(NP):
                        em.inc(ve.scalar_tensor_tensor(
                            ast[:, 8 + p:9 + p], ast[:, 8 + p:9 + p],
                            ast[:, 32 + p:33 + p], ast[:, 40 + p:41 + p],
                            op0=ALU.mult, op1=ALU.add), VE)

                def emit_mixer_tail(b):
                    for p in range(NP):
                        em.wait("pe", SC, sc_exp.get((b, p), 0))
                        em.wait("pe", VE)
                        for i in range(AB // 128):
                            pt = psT1 if i % 2 == 0 else psT2
                            em.inc(pe.transpose(
                                pt[:, 0:128],
                                es_b[:, p * AB + i * 128:p * AB + (i + 1) * 128],
                                ident_bf[:]), PE)
                            em.wait("ve", PE)
                            em.inc(ve.tensor_copy(est_b[:, i * 128:(i + 1) * 128],
                                                  pt[:, 0:128]), VE)
                        em.wait("pe", VE)
                        pp = psE if p % 2 == 0 else psF
                        for lc in range(AB // 128):
                            em.inc(pe.matmul(
                                pp[:, 0:128],
                                est_b[:, lc * 128:(lc + 1) * 128],
                                v_b[:, lc * 1024 + p * 128:
                                    lc * 1024 + p * 128 + 128],
                                start=(lc == 0), stop=(lc == AB // 128 - 1)), PE)
                        em.wait("ve", PE)
                        em.inc(ve.scalar_tensor_tensor(
                            mkv_sb[:, p * 128:(p + 1) * 128],
                            mkv_sb[:, p * 128:(p + 1) * 128],
                            ast[:, 32 + p:33 + p], pp[:, 0:128],
                            op0=ALU.mult, op1=ALU.add), VE)

                for b in range(NAB):
                    t0 = b * AB
                    if b >= 1:
                        emit_mixer_tail(b - 1)
                    emit_stream_ln(em, t0, AB, scr, gp_apply=True,
                                   msb=scr[:, 256:768], rsb=scr[:, 768:1024],
                                   expln=True, castbuf=es_b[:, 0:DC * AB])
                    em.wait("pe", GPC)
                    emit_kv(b)
                    emit_sk_softmax(b)
                emit_mixer_tail(NAB - 1)

                for p in range(NP):
                    em.inc(ve.reciprocal(ast[:, 16 + p:17 + p],
                                         ast[:, 8 + p:9 + p]), VE)
                for p in range(NP):
                    em.inc(ve.tensor_scalar(mkv_sb[:, p * 128:(p + 1) * 128],
                                            mkv_sb[:, p * 128:(p + 1) * 128],
                                            ast[:, 16 + p:17 + p], None,
                                            op0=ALU.mult), VE)
                for p in range(NP):
                    em.inc(ve.tensor_tensor(mkv_bf[:, p * 128:(p + 1) * 128],
                                            mkv_sb[:, p * 128:(p + 1) * 128],
                                            mask_bd[:], op=ALU.mult), VE)

            Em(sems).run(blk, emit)

        def attn_b_phase():
          with nc.sbuf_tensor("b_wq", [128, DC * 1024], BF16) as wq_b, \
             nc.sbuf_tensor("b_wo", [128, DC * 1024], BF16) as wo_b, \
             nc.sbuf_tensor("b_ao", [128, NP * TT], BF16) as ao_b, \
             nc.sbuf_tensor("b_q", [128, 2 * TT], BF16) as q_b, \
             nc.sbuf_tensor("b_qm", [128, 8 * 128], BF16) as qm_sb, \
             nc.sbuf_tensor("b_qmt", [128, 2 * TT], BF16) as qmT, \
             nc.sbuf_tensor("b_st", [128, 48], F32) as bst, \
             nc.sbuf_tensor("b_gscr", [128, 2048], F32) as gscr, \
             nc.Block() as blk:

            def emit(em):
                pe, ve, sc, sy = em.e("pe"), em.e("ve"), em.e("sc"), em.e("sy")
                wqo_cp = {}
                for k in range(DC * 2):
                    dic = k // 2
                    isq = (k % 2 == 0)
                    slot = k % 2
                    bounce = gscr[:, slot * 1024:slot * 1024 + 1024]
                    qeng = "sy" if slot == 0 else "sc"
                    eng = sy if slot == 0 else sc
                    qsem = SW if slot == 0 else WI[1]
                    if k >= 2:
                        em.wait(qeng, VE, wqo_cp[k - 2])
                    src = Wq if isq else Wo
                    wdst = wq_b if isq else wo_b
                    em.inc(eng.dma_start(
                        out=bounce,
                        in_=src[dic * 128:(dic + 1) * 128, :]), qsem)
                    em.wait("ve", qsem)
                    em.inc(ve.tensor_copy(wdst[:, dic * 1024:(dic + 1) * 1024],
                                          bounce), VE)
                    wqo_cp[k] = VE.n if em.mode == "dry" else 0
                w_done = em.snap(VE)
                em.wait("pe", VE, w_done)

                for ti in range(NT):
                    t0 = ti * TT
                    emit_stream_ln(em, t0, TT, gscr[:, 0:512], gp_apply=True,
                                   msb=gscr[:, 512:1536], rsb=gscr[:, 1536:2048],
                                   expln=True, castbuf=ao_b[:, 0:DC * TT])

                    def emit_qmm_pe(p):
                        pb = psA if p % 2 == 0 else psB
                        em.wait("pe", SC)
                        if p == 0:
                            em.wait("pe", GPC)
                            em.wait("pe", VE)
                        for dic in range(DC):
                            em.inc(pe.matmul(
                                pb[:],
                                wq_b[:, dic * 1024 + p * 128:
                                     dic * 1024 + p * 128 + 128],
                                xn_bf[:, dic * TT:(dic + 1) * TT],
                                start=(dic == 0), stop=(dic == DC - 1)), PE)
                        return em.snap(PE)

                    def emit_qmm_sc(p, mmdone):
                        pb = psA if p % 2 == 0 else psB
                        em.wait("sc", PE, mmdone)
                        em.inc(sc.activation(q_b[:, (p % 2) * TT:(p % 2) * TT + TT],
                                             pb[:], AF.Copy), SC)
                        return em.snap(SC)

                    qevacs = {}
                    qmmdone = {}
                    ao_done = {}
                    pet_done = {}
                    for p in range(NP + 1):
                        if p < NP:
                            qmmdone[p] = emit_qmm_pe(p)
                            if p == 0:
                                qevacs[0] = emit_qmm_sc(0, qmmdone[0])
                        if p == 0:
                            continue
                        pp = p - 1
                        par = pp % 2
                        B0, B1 = (psC, psD) if par == 0 else (psE, psF)
                        bb = 24 * par
                        qb0 = par * 512
                        em.wait("pe", SC, qevacs[pp])
                        em.wait("pe", VE, ao_done.get(pp - 2, 0))
                        for tc in range(4):
                            bank = B0 if tc % 2 == 0 else B1
                            em.inc(pe.matmul(
                                bank[:, (tc // 2) * 128:(tc // 2) * 128 + 128],
                                q_b[:, (pp % 2) * TT + tc * 128:
                                    (pp % 2) * TT + tc * 128 + 128],
                                med_bf[:, pp * 128:(pp + 1) * 128],
                                start=True, stop=True), PE)
                        sq_done = em.snap(PE)
                        em.wait("ve", PE, sq_done)
                        for tc in range(4):
                            bank = B0 if tc % 2 == 0 else B1
                            c0 = (tc // 2) * 128
                            em.inc(ve.reduce_max(bst[:, bb + 16 + 2 * tc:
                                                     bb + 17 + 2 * tc],
                                                 bank[:, c0:c0 + 64],
                                                 axis=AX.X, negate=True), VE)
                            em.inc(ve.reduce_max(bst[:, bb + 17 + 2 * tc:
                                                     bb + 18 + 2 * tc],
                                                 bank[:, c0 + 64:c0 + 128],
                                                 axis=AX.X, negate=True), VE)
                        em.wait("sc", VE)
                        em.wait("sc", PE, pet_done.get(pp - 2, 0))
                        for tc in range(4):
                            bank = B0 if tc % 2 == 0 else B1
                            c0 = (tc // 2) * 128
                            em.inc(sc.activation(
                                qm_sb[:, qb0 + tc * 128:qb0 + tc * 128 + 64],
                                bank[:, c0:c0 + 64], AF.Exp,
                                bias=bst[:, bb + 16 + 2 * tc:bb + 17 + 2 * tc],
                                scale=1.0), SC)
                            em.inc(sc.activation(
                                qm_sb[:, qb0 + tc * 128 + 64:qb0 + tc * 128 + 128],
                                bank[:, c0 + 64:c0 + 128], AF.Exp,
                                bias=bst[:, bb + 17 + 2 * tc:bb + 18 + 2 * tc],
                                scale=1.0), SC)
                        exps_done = em.snap(SC)
                        if p < NP:
                            qevacs[p] = emit_qmm_sc(p, qmmdone[p])
                        em.wait("ve", SC, exps_done)
                        for tc in range(4):
                            for hf in range(2):
                                sl = qb0 + tc * 128 + hf * 64
                                em.inc(ve.reduce_sum(
                                    bst[:, bb + 2 * tc + hf:bb + 2 * tc + hf + 1],
                                    qm_sb[:, sl:sl + 64], axis=AX.X), VE)
                        em.inc(ve.tensor_copy(jnk[:, 0:1], jnk[:, 1:2]), VE)
                        em.inc(ve.reciprocal(bst[:, bb + 8:bb + 16],
                                             bst[:, bb:bb + 8]), VE)
                        em.inc(ve.tensor_copy(jnk[:, 1:2], jnk[:, 0:1]), VE)
                        for tc in range(4):
                            for hf in range(2):
                                sl = qb0 + tc * 128 + hf * 64
                                em.inc(ve.tensor_scalar(
                                    qm_sb[:, sl:sl + 64], qm_sb[:, sl:sl + 64],
                                    bst[:, bb + 8 + 2 * tc + hf:
                                        bb + 9 + 2 * tc + hf],
                                    None, op0=ALU.mult), VE)
                        norm_done = em.snap(VE)
                        em.wait("pe", VE, norm_done)
                        for tc in range(4):
                            pt = psT1 if tc % 2 == 0 else psT2
                            c0 = par * 512 + (tc // 2) * 128
                            em.inc(pe.transpose(pt[:, c0:c0 + 128],
                                                qm_sb[:, qb0 + tc * 128:
                                                      qb0 + (tc + 1) * 128],
                                                ident_bf[:]), PE)
                        pet_done[pp] = em.snap(PE)
                        em.wait("ve", PE, pet_done[pp])
                        for tc in range(4):
                            pt = psT1 if tc % 2 == 0 else psT2
                            c0 = par * 512 + (tc // 2) * 128
                            em.inc(ve.tensor_copy(
                                qmT[:, par * 512 + tc * 128:
                                    par * 512 + (tc + 1) * 128],
                                pt[:, c0:c0 + 128]), VE)
                        qmt_done = em.snap(VE)
                        em.wait("pe", VE, qmt_done)
                        em.inc(pe.matmul(B0[:], mkv_bf[:, pp * 128:(pp + 1) * 128],
                                         qmT[:, par * 512:par * 512 + 512],
                                         start=True, stop=True), PE)
                        em.wait("ve", PE)
                        em.inc(ve.tensor_copy(ao_b[:, pp * TT:(pp + 1) * TT],
                                              B0[:]), VE)
                        ao_done[pp] = em.snap(VE)
                    for dc in range(DC):
                        em.wait("pe", VE)
                        pb = psA if dc % 2 == 0 else psB
                        for p in range(NP):
                            em.inc(pe.matmul(
                                pb[:],
                                wo_b[:, p * 1024 + dc * 128:
                                     p * 1024 + dc * 128 + 128],
                                ao_b[:, p * TT:(p + 1) * TT],
                                start=(p == 0), stop=(p == NP - 1)), PE)
                        em.wait("sc", PE)
                        em.inc(sc.activation(gscr[:, 0:512], pb[:], AF.Identity,
                                             bias=bo_c[:, dc:dc + 1],
                                             scale=1.0), SC)
                        em.wait("ve", SC)
                        xsl = x_sb[:, dc * L + t0: dc * L + t0 + TT]
                        em.inc(ve.tensor_tensor(xsl, xsl, gscr[:, 0:512],
                                                op=ALU.add), VE)

            Em(sems).run(blk, emit)

        mkv_bf = None
        med_bf = None
        if _ph in ("fa", "fab", "full"):
            with nc.sbuf_tensor("mkv_bf", [128, NP * 128], BF16) as _t1, \
                 nc.sbuf_tensor("med_bf", [128, NP * 128], BF16) as _t2:
                mkv_bf = _t1
                med_bf = _t2
                attn_a_phase()
                if _ph in ("fab", "full"):
                    attn_b_phase()
        if _ph == "full":
            ffn_phase("f2")

        with nc.Block() as blk:
            def emit(em):
                gp = em.e("gp")
                for dc in range(DC):
                    em.inc(gp.dma_start(out=outT[dc * 128:(dc + 1) * 128, :],
                                        in_=x_sb[:, dc * L:(dc + 1) * L]), SG)
                em.wait("gp", SG)
            Em(sems).run(blk, emit)

    return nc


def _host_consts(inputs):
    import ml_dtypes
    BF = ml_dtypes.bfloat16
    med = np.asarray(inputs["L0_attn_mediator"], np.float32)[0]
    med_bd = np.zeros((NP, 128, 128), np.float32)
    for p in range(NP):
        for hh in range(2):
            h = 2 * p + hh
            med_bd[p, hh * 64:(hh + 1) * 64, hh * 64:(hh + 1) * 64] = \
                med[:, h * 64:(h + 1) * 64].T
    mask = np.zeros((128, 128), np.float32)
    mask[0:64, 0:64] = 1.0
    mask[64:128, 64:128] = 1.0
    ind2 = np.zeros((2, 128), np.float32)
    ind2[0, 0:64] = 1.0
    ind2[1, 64:128] = 1.0
    d = {
        "WfcT": np.ascontiguousarray(np.asarray(inputs["L0_ffn_W_fc"], np.float32).T),
        "WpT": np.ascontiguousarray(np.asarray(inputs["L0_ffn_W_proj"], np.float32).T),
        "Wq": np.ascontiguousarray(np.asarray(inputs["L0_attn_Wq"], np.float32)),
        "Wkv": np.ascontiguousarray(np.asarray(inputs["L0_attn_Wkv"], np.float32)),
        "Wo": np.ascontiguousarray(np.asarray(inputs["L0_attn_Wo"], np.float32)),
        "bo_col": np.ascontiguousarray(
            np.asarray(inputs["L0_attn_bo"], np.float32).reshape(DC, 128).T),
        "lnw_col": np.ascontiguousarray(
            np.asarray(inputs["L0_ffn_ln_w"], np.float32).reshape(HC, 128).T),
        "lnb_col": np.ascontiguousarray(
            np.asarray(inputs["L0_ffn_ln_b"], np.float32).reshape(HC, 128).T),
        "med_bd": med_bd,
        "ident_bf": np.eye(128, dtype=np.float32).astype(BF),
        "ident_f": np.eye(128, dtype=np.float32),
        "mask_bd": mask.astype(BF),
        "ones_misc": np.ones((128, 1), np.float32),
        "ones_row": np.ones((1, 128), np.float32),
        "ind2": ind2,
    }
    return d


def kernel(**inputs):
    x = np.asarray(inputs["L0_attn_x"], np.float32)
    consts = _host_consts(inputs)
    nc = build_program()
    in_maps = []
    for c in range(8):
        m = dict(consts)
        m["xT"] = np.ascontiguousarray(x[c].T)
        in_maps.append(m)
    res = run_bass_kernel_spmd(nc, in_maps, core_ids=list(range(8)))
    out = np.stack([np.ascontiguousarray(r["outT"].T) for r in res.results], axis=0)
    return out
